# version 06
# baseline (speedup 1.0000x reference)
import contextlib
import numpy as np

import concourse.bass as bass
import concourse.mybir as mybir
from concourse.bass_utils import run_bass_kernel_spmd

F32 = mybir.dt.float32
BF16 = mybir.dt.bfloat16
AF = mybir.ActivationFunctionType
ALU = mybir.AluOpType
AX = mybir.AxisListType

S = 4096
D = 1024
NH = 16
HD = 64
WIN = 256
DI = 2730
DEPTH = 6
NEG = -1.0e30
OUTER_C = 12.0
BIG = 1.0e9
NT = S // 128
NSEG = 8
SEG = 512
NJC = (DI + 127) // 128
JLAST = DI - 21 * 128
LN_EPS = 1e-5

SEM_NAMES = ["pe_a", "pe_b", "pe_c", "pe_d",
             "dv_a", "dv_b", "dv_c", "dv_d", "dv_e", "dv_f",
             "ac_a", "ac_b", "ac_c", "ac_d",
             "po_a", "po_b", "gd_a", "gd_b", "yd_a", "yd_b", "yd_c"]


class Sem:
    def __init__(self, handle):
        self.h = handle
        self.n = 0

    def inc(self, instr, k=1):
        instr.then_inc(self.h, k)
        self.n += k
        return self.n


class Ctx:
    def bases(self):
        return {s: getattr(self, s).n for s in SEM_NAMES}


def _band(v, dst, n, lo, hi, c):
    a = c.mskA[:, 0:n]
    b = c.mskB[:, 0:n]
    v.tensor_scalar(a, c.iota_r[:, 0:n], c.iota_c[:], 0.0, ALU.subtract, ALU.add)
    v.tensor_scalar(b, a, float(lo), NEG, ALU.is_lt, ALU.mult)
    v.tensor_scalar(a, a, float(hi), NEG, ALU.is_gt, ALU.mult)
    v.tensor_tensor(dst, a, b, ALU.add)


def build_nc(debug=(), phases=("p1", "p2", "p3", "inner")):
    nc = bass.Bass("TRN2", target_bir_lowering=False,
                   detect_race_conditions=False)
    es = contextlib.ExitStack()
    c = Ctx()
    c.nc = nc
    c.es = es

    def sb(name, shape, dt):
        return es.enter_context(nc.sbuf_tensor(name, shape, dt))

    c.sb = sb

    c.x_in = nc.dram_tensor("x", [S, D], F32, kind="ExternalInput").ap()
    c.W_qkv = nc.dram_tensor("W_qkv", [3 * D, D], F32, kind="ExternalInput").ap()
    c.b_qkv = nc.dram_tensor("b_qkv", [3 * D], F32, kind="ExternalInput").ap()
    c.W_o = nc.dram_tensor("W_o", [D, D], F32, kind="ExternalInput").ap()
    c.b_o = nc.dram_tensor("b_o", [D], F32, kind="ExternalInput").ap()
    c.fWqkv = nc.dram_tensor("fWqkv", [DEPTH, D, 3 * D], F32, kind="ExternalInput").ap()
    c.f_qs = nc.dram_tensor("f_qs", [DEPTH, HD], F32, kind="ExternalInput").ap()
    c.f_ks = nc.dram_tensor("f_ks", [DEPTH, HD], F32, kind="ExternalInput").ap()
    c.fWo = nc.dram_tensor("fWo", [DEPTH, D, D], F32, kind="ExternalInput").ap()
    c.f_lng = nc.dram_tensor("f_lng", [DEPTH, D], F32, kind="ExternalInput").ap()
    c.f_lnb = nc.dram_tensor("f_lnb", [DEPTH, D], F32, kind="ExternalInput").ap()
    c.fW1 = nc.dram_tensor("fW1", [DEPTH, D, 2 * DI], F32, kind="ExternalInput").ap()
    c.f_b1 = nc.dram_tensor("f_b1", [DEPTH, 2 * DI], F32, kind="ExternalInput").ap()
    c.fW2 = nc.dram_tensor("fW2", [DEPTH, DI, D], F32, kind="ExternalInput").ap()
    c.f_b2 = nc.dram_tensor("f_b2", [DEPTH, D], F32, kind="ExternalInput").ap()
    c.out = nc.dram_tensor("out", [S, D], F32, kind="ExternalOutput").ap()

    c.d_QT = nc.dram_tensor("d_QT", [D, S], BF16).ap()
    c.d_KT = nc.dram_tensor("d_KT", [D, S], BF16).ap()
    c.d_V = nc.dram_tensor("d_V", [S, D], BF16).ap()
    c.d_x1 = nc.dram_tensor("d_x1", [S, D], F32).ap()
    c.d_z = [nc.dram_tensor("d_za", [S, D], F32).ap(),
             nc.dram_tensor("d_zb", [S, D], F32).ap()]
    c.d_rows = nc.dram_tensor("d_rows", [4, 32, 128], BF16).ap()
    c.zero_rows = nc.dram_tensor("zero_rows", [4 * 32 * 128], F32,
                                 kind="ExternalInput").ap()
    c.r_cq = nc.dram_tensor("r_cq", [128, S], F32, kind="ExternalInput").ap()
    c.r_sq = nc.dram_tensor("r_sq", [128, S], F32, kind="ExternalInput").ap()
    c.r_ck = nc.dram_tensor("r_ck", [128, S], F32, kind="ExternalInput").ap()
    c.r_sk = nc.dram_tensor("r_sk", [128, S], F32, kind="ExternalInput").ap()

    c.dbg = {}
    for name, shape, dt in debug:
        c.dbg[name] = nc.dram_tensor(name, shape, dt, kind="ExternalOutput").ap()

    for nm in SEM_NAMES:
        setattr(c, nm, Sem(es.enter_context(nc.semaphore(nm))))

    c.identb = sb("identb", [128, 128], BF16)
    c.identf = sb("identf", [128, 128], F32)
    c.iota_c = sb("iota_c", [128, 1], F32)
    c.iota_r = sb("iota_r", [128, 512], F32)
    c.mskA = sb("mskA", [128, 512], F32)
    c.mskB = sb("mskB", [128, 512], F32)
    c.ones_row = sb("ones_row", [1, 1024], BF16)
    c.onesLUT = sb("onesLUT", [128, 65], BF16)
    c.sel16 = sb("sel16", [16, 2048], BF16)
    c.Mqk = sb("Mqk", [128, 1024], BF16)
    c.Mqk0 = sb("Mqk0", [128, 512], BF16)
    c.MT = sb("MT", [128, 1024], BF16)
    c.MT0 = sb("MT0", [128, 512], BF16)
    c.MTdiag = sb("MTdiag", [128, 128], BF16)
    c.cst = sb("cst", [128, 12], F32)
    c.nlut = sb("nlut", [128, 1024], BF16)
    c.bsel = sb("bsel", [128, 1024], BF16)
    c.sgn_col = sb("sgn_col", [128, 1], F32)
    c.ones_f32 = sb("ones_f32", [1, 128], F32)
    c.e64_col = sb("e64_col", [128, 1], F32)
    c.e32_col = sb("e32_col", [128, 1], F32)
    c.Pmat = sb("Pmat", [128, 128], BF16)

    with nc.Block("init") as blk:
        @blk.gpsimd
        def _(g):
            g.iota(c.iota_c[:], pattern=[[1, 1]], base=0, channel_multiplier=1,
                   allow_small_or_imprecise_dtypes=True)
            g.iota(c.iota_r[:], pattern=[[1, 512]], base=0, channel_multiplier=0,
                   allow_small_or_imprecise_dtypes=True)
            g.memset(c.ones_row[:], 1.0)
            c.po_b.inc(g.memset(c.onesLUT[:], 0.0), 1)
            g.wait_ge(c.po_b.h, c.po_b.n)
            g.memset(c.onesLUT[0:64, 0:1], 1.0)
            g.memset(c.onesLUT[64:128, 64:65], 1.0)
            g.memset(c.cst[:, 0:1], LN_EPS)
            g.memset(c.cst[:, 1:2], -0.5)
            g.memset(c.cst[:, 2:3], -OUTER_C)
            g.memset(c.cst[:, 3:4], 0.125)
            g.memset(c.cst[:, 4:5], 8.0)
            g.memset(c.cst[:, 5:6], -np.log(10000.0) / 32.0)
            g.memset(c.cst[:, 6:7], 25.6 / 89.6)
            g.memset(c.cst[:, 7:8], 2.0 / 89.6)
            g.memset(c.cst[:, 8:9], -1.0)
            g.memset(c.cst[:, 9:10], 1.0)
            g.memset(c.cst[:, 10:11], 0.5)
            g.memset(c.ones_f32[:], 1.0)
            c.po_a.inc(g.memset(c.sel16[:], 0.0), 1)

        @blk.vector
        def _(v):
            v.wait_ge(c.po_a.h, c.po_a.n)
            v.tensor_scalar(c.mskA[:, 0:128], c.iota_r[:, 0:128], c.iota_c[:],
                            None, ALU.subtract)
            v.tensor_scalar(c.identf[:], c.mskA[:, 0:128], 0.0, None,
                            ALU.is_equal)
            v.tensor_copy(c.identb[:], c.identf[:])
            v.tensor_scalar(c.mskB[:, 0:128], c.iota_r[:, 0:128], 0.0,
                            c.iota_c[:], ALU.mult, ALU.add)
            for h in range(16):
                v.tensor_scalar(c.sel16[:, 128 * h:128 * (h + 1)],
                                c.mskB[0:16, 0:128], float(h), None, ALU.is_equal)
            for q_ in range(2):
                _band(v, c.Mqk[:, 512 * q_:512 * (q_ + 1)], 512,
                      128 * q_, 128 * q_ + 256, c)
            for q_ in range(2):
                _band(v, c.Mqk0[:, 256 * q_:256 * (q_ + 1)], 256, -BIG, 128 * q_, c)
            for q_ in range(2):
                for cc in range(4):
                    _band(v, c.MT[:, 512 * q_ + 128 * cc:512 * q_ + 128 * (cc + 1)],
                          128, 128 * (cc - q_) - 256, 128 * (cc - q_), c)
            for q_ in range(2):
                for cc in range(2):
                    _band(v, c.MT0[:, 256 * q_ + 128 * cc:256 * q_ + 128 * (cc + 1)],
                          128, 128 * (cc - q_), BIG, c)
            _band(v, c.MTdiag[:], 128, 0, BIG, c)
            half_lo = c.mskA[:, 1:2]
            v.tensor_scalar(half_lo, c.iota_c[:], 64.0, None, ALU.is_lt)
            half_hi = c.mskA[:, 2:3]
            v.tensor_scalar(half_hi, c.iota_c[:], 64.0, None, ALU.is_ge)
            for fc in range(8):
                a = c.mskB[:, 0:128]
                b = c.mskB[:, 128:256]
                v.tensor_scalar(a, c.iota_r[:, 0:128], float(fc), None, ALU.is_equal)
                v.tensor_scalar(a, a, half_lo, None, ALU.mult)
                v.tensor_scalar(b, c.iota_r[:, 0:128], float(64 + fc), None,
                                ALU.is_equal)
                v.tensor_scalar(b, b, half_hi, None, ALU.mult)
                v.tensor_tensor(c.nlut[:, 128 * fc:128 * (fc + 1)], a, b, ALU.add)
                pa = c.mskA[:, 3:4]
                v.tensor_scalar(pa, c.iota_c[:], float(fc), None, ALU.is_equal)
                v.tensor_scalar(a, c.iota_r[:, 0:128], 64.0, None, ALU.is_lt)
                v.tensor_scalar(a, a, pa, None, ALU.mult)
                rb = c.mskB[:, 256:384]
                v.tensor_scalar(pa, c.iota_c[:], float(64 + fc), None, ALU.is_equal)
                v.tensor_scalar(rb, c.iota_r[:, 0:128], 64.0, None, ALU.is_ge)
                v.tensor_scalar(rb, rb, pa, None, ALU.mult)
                v.tensor_tensor(c.bsel[:, 128 * fc:128 * (fc + 1)], a, rb, ALU.add)
            v.tensor_scalar(c.mskA[:, 0:128], c.iota_r[:, 0:128], 64.0, -64.0,
                            ALU.is_ge, ALU.mult)
            v.tensor_tensor(c.mskA[:, 0:128], c.iota_r[:, 0:128],
                            c.mskA[:, 0:128], ALU.add)
            v.tensor_scalar(c.mskA[:, 0:128], c.mskA[:, 0:128], 32.0, -64.0,
                            ALU.is_ge, ALU.mult)
            v.tensor_tensor(c.mskA[:, 0:128], c.iota_r[:, 0:128],
                            c.mskA[:, 0:128], ALU.add)
            v.tensor_scalar(c.mskA[:, 0:128], c.mskA[:, 0:128], 32.0, None,
                            ALU.add)
            v.tensor_scalar(c.mskA[:, 0:128], c.mskA[:, 0:128], c.iota_c[:],
                            None, ALU.subtract)
            v.tensor_scalar(c.Pmat[:], c.mskA[:, 0:128], 0.0, None,
                            ALU.is_equal)
            c.dv_a.inc(v.tensor_copy(c.mskB[:, 0:1], c.iota_c[:]), 1)
    c.init_done = c.dv_a.n

    if "p1" in phases:
        p1_outer_qkv(c)
    if "p2" in phases:
        p2_outer_attn(c)
    if "p3" in phases:
        p3_outer_proj(c)
    if "inner" in phases:
        for li in range(DEPTH):
            pa_inner_attn(c, li)
            pb_inner_ffn(c, li, final=(li == DEPTH - 1))
    else:
        for ph in phases:
            if ph.startswith("pa"):
                pa_inner_attn(c, int(ph[2:]))
            elif ph.startswith("pb"):
                pb_inner_ffn(c, int(ph[2:].rstrip("f")),
                             final=(ph.endswith("f")))
    _final_wait(c)

    if hasattr(c, "p23_stack"):
        c.p23_stack.close()
    es.close()
    return nc, c


def p1_outer_qkv(c):
    upto = getattr(c, 'p1_upto', 4)
    nc = c.nc
    st = contextlib.ExitStack()

    def sb(name, shape, dt):
        return st.enter_context(nc.sbuf_tensor(name, shape, dt))

    def psb(name, shape, dt=F32):
        return st.enter_context(nc.psum_tensor(name, shape, dt))

    Wraw = sb("p1_Wraw", [128, 24 * D], BF16)
    WT = sb("p1_WT", [128, 24 * D], BF16)
    brow = sb("p1_brow", [1, 3 * D], BF16)
    xt = [sb(f"p1_x{i}", [128, D], F32) for i in range(4)]
    xn = [sb(f"p1_xn{i}", [128, D], BF16) for i in range(4)]
    fm = sb("p1_fm", [128, 8 * SEG], BF16)
    ev = [sb(f"p1_ev{i}", [128, SEG], BF16) for i in range(4)]
    vev = [sb(f"p1_vev{i}", [128, D], BF16) for i in range(2)]
    statg = sb("p1_stat", [128, 28], F32)
    sq = sb("p1_sq", [128, D], F32)

    ps_t = [psb(f"p1_pst{i}", [128, 1024], BF16) for i in range(2)]
    ps_qk = [psb(f"p1_psqk{i}", [128, 512]) for i in range(2)]
    ps_v = [psb(f"p1_psv{i}", [128, D]) for i in range(2)]

    B = c.bases()
    with nc.Block("p1w") as blk:
        @blk.gpsimd
        def _(g):
            c.gd_a.inc(g.dma_start(Wraw[:].rearrange("p (c d) -> p c d", c=24),
                                   c.W_qkv.rearrange("(c p) d -> p c d", p=128)), 16)
            c.gd_a.inc(g.dma_start(brow[0:1, :], c.b_qkv[None, :]), 16)

        @blk.tensor
        def _(t):
            t.wait_ge(c.gd_a.h, B["gd_a"] + 32)
            t.wait_ge(c.dv_a.h, c.init_done)
            gi = 0
            for dc in range(8):
                for fcg in range(6):
                    if gi >= 2:
                        t.wait_ge(c.dv_c.h, B["dv_c"] + gi - 1)
                    for k in range(4):
                        fc = 4 * fcg + k
                        ins = nc.tensor.transpose(
                            ps_t[gi % 2][:, 128 * k:128 * (k + 1)],
                            Wraw[:, 1024 * fc + 128 * dc:1024 * fc + 128 * (dc + 1)],
                            c.identb[:])
                        if k == 3:
                            c.pe_a.inc(ins, 1)
                    gi += 1

        @blk.vector
        def _(v):
            for gi in range(48):
                dc, fcg = gi // 6, gi % 6
                v.wait_ge(c.pe_a.h, B["pe_a"] + gi + 1)
                c.dv_c.inc(v.tensor_copy(
                    WT[:, 3072 * dc + 512 * fcg:3072 * dc + 512 * (fcg + 1)],
                    ps_t[gi % 2][:, 0:512]), 1)

    B = c.bases()

    def qk_store_cnt(g_, fc):
        return 16 * (20 * g_ + fc + 1)

    def v_store_cnt(tau):
        return 16 * (20 * (tau // 4) + 17 + (tau % 4))

    if upto < 2:
        if "dbg_wt" in c.dbg:
            with nc.Block("p1dbgw") as blk:
                @blk.gpsimd
                def _(g):
                    g.wait_ge(c.dv_c.h, c.dv_c.n)
                    c.gd_b.inc(g.dma_start(c.dbg["dbg_wt"][:], WT[:]), 16)
                    g.wait_ge(c.gd_b.h, c.gd_b.n)
        st.close()
        return
    with nc.Block("p1m") as blk:
        if "dbg_xn" in c.dbg:
            @blk.gpsimd
            def _(g):
                g.wait_ge(c.ac_b.h, B["ac_b"] + 4)
                for i in range(4):
                    c.gd_b.inc(g.dma_start(
                        c.dbg["dbg_xn"][128 * i:128 * (i + 1), :], xn[i][:]), 16)
                c.gd_b.inc(g.dma_start(c.dbg["dbg_st"][:], statg[:]), 16)
                g.wait_ge(c.gd_b.h, c.gd_b.n)

        @blk.sync
        def _(sy):
            for ti in range(4):
                c.yd_a.inc(sy.dma_start(xt[ti][:], c.x_in[128 * ti:128 * (ti + 1), :]),
                           16)
            for g_ in range(NSEG):
                for fc in (range(16) if upto >= 4 else []):
                    dst = c.d_QT if fc < 8 else c.d_KT
                    r0 = 128 * (fc % 8)
                    if fc % 2 == 0:
                        sy.wait_ge(c.dv_d.h, B["dv_d"] + 8 * g_ + fc // 2 + 1)
                    else:
                        sy.wait_ge(c.ac_c.h, B["ac_c"] + 8 * g_ + fc // 2 + 1)
                    c.yd_b.inc(sy.dma_start(
                        dst[r0:r0 + 128, SEG * g_:SEG * (g_ + 1)], ev[fc % 4][:]), 16)
                for ti in (range(4) if upto >= 4 else []):
                    tau = 4 * g_ + ti
                    sy.wait_ge(c.dv_e.h, B["dv_e"] + tau + 1)
                    c.yd_b.inc(sy.dma_start(
                        c.d_V[128 * tau:128 * (tau + 1), :], vev[tau % 2][:]), 16)
                if g_ + 1 < NSEG:
                    for ti in range(4):
                        tau = 4 * (g_ + 1) + ti
                        if upto >= 2:
                            sy.wait_ge(c.ac_b.h, B["ac_b"] + tau - 4 + 1)
                        c.yd_a.inc(sy.dma_start(
                            xt[tau % 4][:], c.x_in[128 * tau:128 * (tau + 1), :]), 16)

        @blk.vector
        def _(v):
            for g_ in range(NSEG):
                for ti in range(4):
                    tau = 4 * g_ + ti
                    v.wait_ge(c.yd_a.h, B["yd_a"] + 16 * (tau + 1))
                    x_ = xt[tau % 4]
                    stt = statg[:, 7 * ti:7 * ti + 7]
                    v.tensor_reduce(stt[:, 0:1], x_[:], AX.X, ALU.add)
                    v.scalar_tensor_tensor(sq[:], x_[:], 1.0, x_[:], ALU.mult,
                                           ALU.mult, accum_out=stt[:, 1:2])
                    v.tensor_scalar(stt[:, 2:3], stt[:, 0:1], 1.0 / D, None, ALU.mult)
                    v.tensor_scalar(stt[:, 3:4], stt[:, 1:2], 1.0 / D, None, ALU.mult)
                    v.tensor_tensor(stt[:, 4:5], stt[:, 2:3], stt[:, 2:3], ALU.mult)
                    v.tensor_copy(sq[:, 0:1], c.iota_c[:])
                    c.dv_a.inc(v.tensor_tensor(stt[:, 3:4], stt[:, 3:4], stt[:, 4:5],
                                               ALU.subtract), 1)
                    v.wait_ge(c.ac_a.h, B["ac_a"] + tau + 1)
                    c.dv_b.inc(v.tensor_scalar(stt[:, 6:7], stt[:, 2:3],
                                               stt[:, 5:6], -1.0, ALU.mult,
                                               ALU.mult), 1)
                for dc in (range(8) if upto >= 3 else []):
                    v.wait_ge(c.pe_a.h, B["pe_a"] + 8 * g_ + dc + 1)
                    c.dv_c.inc(v.tensor_copy(fm[:, 512 * dc:512 * (dc + 1)],
                                             ps_t[dc % 2][:, 0:512]), 1)
                for fc in (range(0, 16, 2) if upto >= 4 else []):
                    v.wait_ge(c.pe_b.h, B["pe_b"] + 16 * g_ + fc + 1)
                    if fc >= 4 or g_ > 0:
                        pf = qk_store_cnt(g_, fc - 4) if fc >= 4 \
                            else qk_store_cnt(g_ - 1, 16 + 3)
                        v.wait_ge(c.yd_b.h, B["yd_b"] + pf)
                    c.dv_d.inc(v.tensor_copy(ev[fc % 4][:], ps_qk[0][:]), 1)
                for ti in (range(4) if upto >= 4 else []):
                    tau = 4 * g_ + ti
                    v.wait_ge(c.pe_c.h, B["pe_c"] + tau + 1)
                    if tau >= 2:
                        v.wait_ge(c.yd_b.h, B["yd_b"] + v_store_cnt(tau - 2))
                    c.dv_e.inc(v.tensor_copy(vev[tau % 2][:], ps_v[tau % 2][:]), 1)

        @blk.scalar
        def _(sc):
            for g_ in range(NSEG):
                for ti in range(4):
                    tau = 4 * g_ + ti
                    stt = statg[:, 7 * ti:7 * ti + 7]
                    sc.wait_ge(c.dv_a.h, B["dv_a"] + tau + 1)
                    sc.activation(stt[:, 5:6], stt[:, 3:4], AF.Ln, bias=c.cst[:, 0:1],
                                  scale=1.0)
                    sc.activation(stt[:, 5:6], stt[:, 5:6], AF.Exp,
                                  bias=0.0, scale=c.cst[:, 1:2])
                    c.ac_a.inc(sc.drain(), 1)
                    sc.wait_ge(c.dv_b.h, B["dv_b"] + tau + 1)
                    if tau >= 4 and upto >= 3:
                        sc.wait_ge(c.pe_a.h, B["pe_a"] + 8 * (tau // 4))
                    sc.activation(xn[tau % 4][:], xt[tau % 4][:],
                                  AF.Identity, bias=stt[:, 6:7],
                                  scale=stt[:, 5:6])
                    c.ac_b.inc(sc.drain(), 1)
                for fc in (range(1, 16, 2) if upto >= 4 else []):
                    sc.wait_ge(c.pe_b.h, B["pe_b"] + 16 * g_ + fc + 1)
                    if fc >= 4 or g_ > 0:
                        pf = qk_store_cnt(g_, fc - 4) if fc >= 4 \
                            else qk_store_cnt(g_ - 1, 16 + 3)
                        sc.wait_ge(c.yd_b.h, B["yd_b"] + pf)
                    sc.copy(ev[fc % 4][:], ps_qk[1][:])
                    c.ac_c.inc(sc.drain(), 1)

        @blk.tensor
        def _(t):
            if upto < 3:
                return
            t.wait_ge(c.dv_c.h, B["dv_c"])
            for g_ in range(NSEG):
                for dc in range(8):
                    gi = 8 * g_ + dc
                    if gi >= 2:
                        t.wait_ge(c.dv_c.h, B["dv_c"] + gi - 1)
                    for ti in range(4):
                        tau = 4 * g_ + ti
                        t.wait_ge(c.ac_b.h, B["ac_b"] + tau + 1)
                        ins = nc.tensor.transpose(
                            ps_t[dc % 2][:, 128 * ti:128 * (ti + 1)],
                            xn[tau % 4][:, 128 * dc:128 * (dc + 1)], c.identb[:])
                        if ti == 3:
                            c.pe_a.inc(ins, 1)
                if upto < 4:
                    continue
                t.wait_ge(c.dv_c.h, B["dv_c"] + 8 * (g_ + 1))
                for fc in range(16):
                    pq = ps_qk[fc % 2]
                    if 16 * g_ + fc >= 2:
                        idx = 16 * g_ + fc - 2
                        if fc % 2 == 0:
                            t.wait_ge(c.dv_d.h, B["dv_d"] + idx // 2 + 1)
                        else:
                            t.wait_ge(c.ac_c.h, B["ac_c"] + idx // 2 + 1)
                    for dc in range(8):
                        nc.tensor.matmul(
                            pq[:], WT[:, 3072 * dc + 128 * fc:3072 * dc + 128 * (fc + 1)],
                            fm[:, 512 * dc:512 * (dc + 1)],
                            start=(dc == 0), stop=False)
                    c.pe_b.inc(nc.tensor.matmul(
                        pq[:], brow[0:1, 128 * fc:128 * (fc + 1)],
                        c.ones_row[0:1, 0:512], start=False, stop=True), 1)
                for ti in range(4):
                    tau = 4 * g_ + ti
                    pv = ps_v[tau % 2]
                    if tau >= 2:
                        t.wait_ge(c.dv_e.h, B["dv_e"] + tau - 1)
                    for half in range(2):
                        col0 = 2048 + 512 * half
                        for dc in range(8):
                            nc.tensor.matmul(
                                pv[:, 512 * half:512 * (half + 1)],
                                fm[:, 512 * dc + 128 * ti:512 * dc + 128 * (ti + 1)],
                                WT[:, 3072 * dc + col0:3072 * dc + col0 + 512],
                                start=(dc == 0), stop=False)
                        ins = nc.tensor.matmul(
                            pv[:, 512 * half:512 * (half + 1)],
                            c.ones_row[0:1, 0:128], brow[0:1, col0:col0 + 512],
                            start=False, stop=True)
                        if half == 1:
                            c.pe_c.inc(ins, 1)

    if "dbg_qt" in c.dbg:
        with nc.Block("p1dbg") as blk:
            @blk.gpsimd
            def _(g):
                g.wait_ge(c.yd_b.h, c.yd_b.n)
                c.gd_b.inc(g.dma_start(c.dbg["dbg_qt"][:], c.d_QT[:]), 16)
                c.gd_b.inc(g.dma_start(c.dbg["dbg_kt"][:], c.d_KT[:]), 16)
                c.gd_b.inc(g.dma_start(c.dbg["dbg_v"][:], c.d_V[:]), 16)
                g.wait_ge(c.gd_b.h, c.gd_b.n)

    st.close()
    c.p1_done = dict(yd_b=c.yd_b.n)


def p2_outer_attn(c):
    nc = c.nc
    st = contextlib.ExitStack()

    def sb(name, shape, dt):
        return st.enter_context(nc.sbuf_tensor(name, shape, dt))

    def psb(name, shape, dt=F32):
        return st.enter_context(nc.psum_tensor(name, shape, dt))

    c.p23_stack = contextlib.ExitStack()
    VB = 194
    c.outT = c.p23_stack.enter_context(
        nc.sbuf_tensor("p2_outT", [128, 8 * S], BF16))
    KT = [sb(f"p2_kt{i}", [128, S], BF16) for i in range(2)]
    QT = [sb(f"p2_qt{i}", [128, S], BF16) for i in range(2)]
    VS = [sb(f"p2_vs{i}", [128, 32 * VB], BF16) for i in range(2)]
    pT = [sb(f"p2_pt{i}", [128, 1024], BF16) for i in range(2)]
    rc = sb("p2_rc", [1, 256], F32)
    bcs = sb("p2_bcs", [128, 256], F32)
    onesA = sb("p2_onesA", [1, 64], F32)
    onesB = sb("p2_onesB", [1, 128], F32)

    ps_sT = [psb(f"p2_pssT{i}", [128, 1024]) for i in range(2)]
    ps_oA = psb("p2_psoA", [65, 512])
    ps_oB = psb("p2_psoB", [128, 512])
    ps_bc = psb("p2_psbc", [128, 512])
    dbgS = sb("p2_dbgS", [128, 1280], F32) if "dbg_p2s" in c.dbg else None

    B = c.bases()
    ngrp = [(qt // 4) + 1 for qt in range(32)]
    grp_pref = [0]
    for qt in range(32):
        grp_pref.append(grp_pref[-1] + ngrp[qt])
    GPP = grp_pref[32]

    with nc.Block("p2") as blk:
        @blk.gpsimd
        def _(g):
            g.memset(VS[0][:], 0.0)
            g.memset(VS[1][:], 0.0)
            g.memset(VS[0][:].rearrange("p (k c) -> p k c", c=VB)[:, :, 32:33], 1.0)
            g.memset(VS[0][:].rearrange("p (k c) -> p k c", c=VB)[:, :, 193:194], 1.0)
            g.memset(VS[1][:].rearrange("p (k c) -> p k c", c=VB)[:, :, 32:33], 1.0)
            g.memset(VS[1][:].rearrange("p (k c) -> p k c", c=VB)[:, :, 193:194], 1.0)
            g.memset(onesA[:], 1.0)
            g.memset(onesB[0:1, 0:64], 0.0)
            c.po_a.inc(g.memset(onesB[0:1, 64:128], 1.0), 1)

        @blk.sync
        def _(sy):
            sy.wait_ge(c.yd_b.h, B["yd_b"])
            for p in range(8):
                if p >= 2:
                    sy.wait_ge(c.dv_b.h, B["dv_b"] + 32 * (p - 1))
                sy.wait_ge(c.po_a.h, B["po_a"] + 1)
                c.yd_a.inc(sy.dma_start(KT[p % 2][:],
                                        c.d_KT[128 * p:128 * (p + 1), :]), 16)
                c.yd_a.inc(sy.dma_start(QT[p % 2][:],
                                        c.d_QT[128 * p:128 * (p + 1), :]), 16)
                va = VS[p % 2][:].rearrange("p (k c) -> p k c", c=VB)
                c.yd_a.inc(sy.dma_start(
                    va[:, :, 129:193],
                    c.d_V.rearrange("(k p) e -> p k e", p=128)[:, :, 128 * p:128 * p + 64]), 16)
                c.yd_a.inc(sy.dma_start(
                    va[:, :, 64:128],
                    c.d_V.rearrange("(k p) e -> p k e", p=128)[:, :, 128 * p + 64:128 * p + 128]), 16)

        @blk.tensor
        def _(t):
            t.wait_ge(c.po_a.h, B["po_a"] + 1)
            gcnt = 0
            pvcnt = 0
            for p in range(8):
                t.wait_ge(c.yd_a.h, B["yd_a"] + 64 * (p + 1))
                for qt in range(32):
                    for gi in range(ngrp[qt]):
                        kcs = range(4 * gi, min(4 * gi + 4, qt + 1))
                        pss = ps_sT[gcnt % 2]
                        if gcnt >= 2:
                            t.wait_ge(c.ac_a.h, B["ac_a"] + gcnt - 1)
                        first = True
                        for kc in kcs:
                            ca = 128 * (kc % 4)
                            if kc == qt:
                                t.matmul(pss[:, ca:ca + 128], c.identb[:],
                                         c.MTdiag[:], start=True, stop=False)
                                t.matmul(pss[:, 512 + ca:512 + ca + 128],
                                         c.identb[:], c.MTdiag[:],
                                         start=True, stop=False)
                                st_ = False
                            else:
                                st_ = True
                            t.matmul(pss[:, ca:ca + 128],
                                     KT[p % 2][0:64, 128 * kc:128 * (kc + 1)],
                                     QT[p % 2][0:64, 128 * qt:128 * (qt + 1)],
                                     start=st_, stop=True, tile_position=(0, 0))
                            ins = t.matmul(pss[:, 512 + ca:512 + ca + 128],
                                           KT[p % 2][64:128, 128 * kc:128 * (kc + 1)],
                                           QT[p % 2][64:128, 128 * qt:128 * (qt + 1)],
                                           start=st_, stop=True,
                                           tile_position=(64, 0))
                        c.pe_b.inc(ins, 1)
                        gcnt += 1
                        if gi == 0:
                            t.wait_ge(c.dv_b.h, B["dv_b"] + 32 * p + qt)
                        t.wait_ge(c.ac_a.h, B["ac_a"] + gcnt)
                        va = VS[p % 2][:].rearrange("p (k c) -> p k c", c=VB)
                        for kc in kcs:
                            ca = 128 * (kc % 4)
                            t.matmul(ps_oA[:, 0:128], va[:, kc, 129:194],
                                     pT[(gcnt - 1) % 2][:, ca:ca + 128],
                                     start=(kc == kcs[0] and gi == 0),
                                     stop=(kc == qt))
                            ins = t.matmul(ps_oB[:, 0:128], va[:, kc, 0:128],
                                           pT[(gcnt - 1) % 2][:, 512 + ca:512 + ca + 128],
                                           start=(kc == kcs[0] and gi == 0),
                                           stop=(kc == qt))
                        c.pe_c.inc(ins, 1)
                        pvcnt += 1
                    iq = 32 * p + qt
                    t.wait_ge(c.dv_a.h, B["dv_a"] + iq + 1)
                    t.matmul(ps_bc[0:64, 0:128], onesA[:], rc[0:1, 0:128],
                             start=True, stop=True)
                    c.pe_d.inc(t.matmul(ps_bc[:, 128:256], onesB[:],
                                        rc[0:1, 128:256], start=True, stop=True), 1)

        @blk.scalar
        def _(sc):
            gcnt = 0
            for p in range(8):
                for qt in range(32):
                    for gi in range(ngrp[qt]):
                        klen = min(4 * gi + 4, qt + 1) - 4 * gi
                        sc.wait_ge(c.pe_b.h, B["pe_b"] + gcnt + 1)
                        if gcnt >= 2:
                            sc.wait_ge(c.pe_c.h, B["pe_c"] + gcnt - 1)
                        po = pT[gcnt % 2][:].rearrange("p (g x) -> p g x", g=2)
                        pi = ps_sT[gcnt % 2][:].rearrange("p (g x) -> p g x", g=2)
                        sc.activation(po[:, :, 0:128 * klen], pi[:, :, 0:128 * klen],
                                      AF.Exp, bias=c.cst[:, 2:3], scale=c.cst[:, 3:4])
                        c.ac_a.inc(sc.drain(), 1)
                        gcnt += 1

        @blk.vector
        def _(v):
            for p in range(8):
                for qt in range(32):
                    iq = 32 * p + qt
                    if iq == 0 and "dbg_p2s" in c.dbg:
                        v.wait_ge(c.pe_b.h, B["pe_b"] + 1)
                        v.tensor_copy(dbgS[:, 0:128], ps_sT[0][:, 0:128])
                        v.tensor_copy(dbgS[:, 128:256], ps_sT[0][:, 512:640])
                        v.wait_ge(c.ac_a.h, B["ac_a"] + 1)
                        v.tensor_copy(dbgS[:, 256:384], pT[0][:, 0:128])
                        v.tensor_copy(dbgS[:, 384:512], pT[0][:, 512:640])
                    v.wait_ge(c.pe_c.h, B["pe_c"] + grp_pref[qt + 1] + GPP * p)
                    if iq >= 1:
                        v.wait_ge(c.pe_d.h, B["pe_d"] + iq)
                    if iq == 0 and "dbg_p2s" in c.dbg:
                        v.tensor_copy(dbgS[0:65, 512:640], ps_oA[0:65, 0:128])
                        v.tensor_copy(dbgS[:, 640:768], ps_oB[:, 0:128])
                    v.reciprocal(rc[0:1, 0:128], ps_oA[64:65, 0:128])
                    c.dv_a.inc(v.reciprocal(rc[0:1, 128:256], ps_oB[32:33, 0:128]), 1)
                    if iq == 0 and "dbg_p2s" in c.dbg:
                        v.tensor_copy(dbgS[0:1, 768:1024], rc[0:1, 0:256])
                    v.wait_ge(c.pe_d.h, B["pe_d"] + iq + 1)
                    if iq == 0 and "dbg_p2s" in c.dbg:
                        v.tensor_copy(dbgS[:, 1024:1280], ps_bc[:, 0:256])
                    v.tensor_copy(bcs[0:64, 0:128], ps_bc[0:64, 0:128])
                    v.tensor_copy(bcs[64:128, 128:256], ps_bc[64:128, 128:256])
                    o = c.outT[:, 4096 * p + 128 * qt:4096 * p + 128 * (qt + 1)]
                    v.tensor_tensor(o[0:64, :], ps_oA[0:64, 0:128], bcs[0:64, 0:128],
                                    ALU.mult)
                    c.dv_b.inc(v.tensor_tensor(o[64:128, :], ps_oB[64:128, 0:128],
                                               bcs[64:128, 128:256], ALU.mult), 1)

    if "dbg_ot" in c.dbg:
        with nc.Block("p2dbg") as blk:
            @blk.gpsimd
            def _(g):
                g.wait_ge(c.dv_b.h, c.dv_b.n)
                c.gd_b.inc(g.dma_start(c.dbg["dbg_ot"][:], c.outT[:]), 16)
                if "dbg_p2s" in c.dbg:
                    c.gd_b.inc(g.dma_start(c.dbg["dbg_p2s"][:], dbgS[:]), 16)
                g.wait_ge(c.gd_b.h, c.gd_b.n)

    st.close()


def p3_outer_proj(c):
    nc = c.nc
    st = contextlib.ExitStack()

    def sb(name, shape, dt):
        return st.enter_context(nc.sbuf_tensor(name, shape, dt))

    def psb(name, shape, dt=F32):
        return st.enter_context(nc.psum_tensor(name, shape, dt))

    WoT = sb("p3_wot", [128, 8 * D], BF16)
    Wraw = sb("p3_wraw", [128, 8 * D], BF16)
    borow = sb("p3_bo", [1, D], BF16)
    xt = [sb(f"p3_x{i}", [128, D], F32) for i in range(2)]
    ysb = [sb(f"p3_y{i}", [128, D], F32) for i in range(2)]
    x1t = [sb(f"p3_x1{i}", [128, D], F32) for i in range(2)]
    z0t = [sb(f"p3_z0{i}", [128, D], F32) for i in range(2)]
    stat = sb("p3_stat", [128, 16], F32)
    sq = sb("p3_sq", [128, D], F32)
    ps_t = [psb(f"p3_pst{i}", [128, 1024], BF16) for i in range(2)]
    ps_o = [psb(f"p3_pso{i}", [128, D]) for i in range(2)]

    B = c.bases()
    with nc.Block("p3") as blk:
        @blk.gpsimd
        def _(g):
            c.gd_a.inc(g.dma_start(Wraw[:].rearrange("p (c d) -> p c d", c=8),
                                   c.W_o.rearrange("(c p) d -> p c d", p=128)), 16)
            c.gd_a.inc(g.dma_start(borow[0:1, :], c.b_o[None, :]), 16)

        @blk.tensor
        def _(t):
            t.wait_ge(c.gd_a.h, c.gd_a.n)
            gi = 0
            for dc in range(8):
                for ec in range(8):
                    if gi >= 2:
                        t.wait_ge(c.dv_a.h, B["dv_a"] + gi - 1)
                    ins = nc.tensor.transpose(
                        ps_t[gi % 2][:, 0:128],
                        Wraw[:, 1024 * dc + 128 * ec:1024 * dc + 128 * (ec + 1)],
                        c.identb[:])
                    c.pe_a.inc(ins, 1)
                    gi += 1
            for tau in range(NT):
                po = ps_o[tau % 2]
                if tau >= 2:
                    t.wait_ge(c.ac_a.h, B["ac_a"] + tau - 1)
                for half in range(2):
                    for ec in range(8):
                        t.matmul(po[:, 512 * half:512 * (half + 1)],
                                 c.outT[:, 4096 * ec + 128 * tau:4096 * ec + 128 * (tau + 1)],
                                 WoT[:, 1024 * ec + 512 * half:1024 * ec + 512 * (half + 1)],
                                 start=(ec == 0), stop=False)
                    ins = t.matmul(po[:, 512 * half:512 * (half + 1)],
                                   c.ones_row[0:1, 0:128],
                                   borow[0:1, 512 * half:512 * (half + 1)],
                                   start=False, stop=True)
                c.pe_b.inc(ins, 1)

        @blk.vector
        def _(v):
            gi = 0
            for dc in range(8):
                for ec in range(8):
                    v.wait_ge(c.pe_a.h, B["pe_a"] + gi + 1)
                    c.dv_a.inc(v.tensor_copy(
                        WoT[:, 1024 * ec + 128 * dc:1024 * ec + 128 * (dc + 1)],
                        ps_t[gi % 2][:, 0:128]), 1)
                    gi += 1
            for tau in range(NT):
                stt = stat[:, 0:16]
                v.wait_ge(c.ac_a.h, B["ac_a"] + tau + 1)
                y = ysb[tau % 2]
                v.tensor_reduce(stt[:, 0:1], y[:], AX.X, ALU.add)
                v.scalar_tensor_tensor(sq[:], y[:], 1.0, y[:], ALU.mult, ALU.mult,
                                       accum_out=stt[:, 1:2])
                v.tensor_scalar(stt[:, 2:3], stt[:, 0:1], 1.0 / D, None, ALU.mult)
                v.tensor_scalar(stt[:, 3:4], stt[:, 1:2], 1.0 / D, None, ALU.mult)
                v.tensor_tensor(stt[:, 4:5], stt[:, 2:3], stt[:, 2:3], ALU.mult)
                v.tensor_copy(sq[:, 0:1], c.iota_c[:])
                c.dv_b.inc(v.tensor_tensor(stt[:, 3:4], stt[:, 3:4], stt[:, 4:5],
                                           ALU.subtract), 1)
                v.wait_ge(c.ac_b.h, B["ac_b"] + tau + 1)
                c.dv_c.inc(v.tensor_scalar(stt[:, 6:7], stt[:, 2:3], stt[:, 5:6],
                                           -1.0, ALU.mult, ALU.mult), 1)
                v.wait_ge(c.ac_c.h, B["ac_c"] + tau + 1)
                v.wait_ge(c.yd_a.h, B["yd_a"] + 16 * (tau + 1))
                if tau >= 2:
                    v.wait_ge(c.yd_b.h, B["yd_b"] + 16 * (2 * (tau - 2) + 1))
                    v.wait_ge(c.ac_d.h, B["ac_d"] + 2 * (tau - 2) + 2)
                c.dv_d.inc(v.tensor_tensor(x1t[tau % 2][:], z0t[tau % 2][:],
                                           xt[tau % 2][:], ALU.add), 1)
                x1 = x1t[tau % 2]
                v.tensor_reduce(stt[:, 8:9], x1[:], AX.X, ALU.add)
                v.scalar_tensor_tensor(sq[:], x1[:], 1.0, x1[:], ALU.mult, ALU.mult,
                                       accum_out=stt[:, 9:10])
                v.tensor_scalar(stt[:, 10:11], stt[:, 8:9], 1.0 / D, None, ALU.mult)
                v.tensor_scalar(stt[:, 11:12], stt[:, 9:10], 1.0 / D, None, ALU.mult)
                v.tensor_tensor(stt[:, 12:13], stt[:, 10:11], stt[:, 10:11], ALU.mult)
                v.tensor_copy(sq[:, 0:1], c.iota_c[:])
                c.dv_e.inc(v.tensor_tensor(stt[:, 11:12], stt[:, 11:12],
                                           stt[:, 12:13], ALU.subtract), 1)
                v.wait_ge(c.ac_d.h, B["ac_d"] + 2 * tau + 1)
                c.dv_f.inc(v.tensor_scalar(stt[:, 14:15], stt[:, 10:11],
                                           stt[:, 13:14], -1.0, ALU.mult, ALU.mult), 1)

        @blk.scalar
        def _(sc):
            for tau in range(NT):
                sc.wait_ge(c.pe_b.h, B["pe_b"] + tau + 1)
                if tau >= 2:
                    sc.wait_ge(c.dv_b.h, B["dv_b"] + tau - 1)
                sc.activation(ysb[tau % 2][:], ps_o[tau % 2][:], AF.Identity,
                              bias=0.0, scale=1.0)
                c.ac_a.inc(sc.drain(), 1)
                stt = stat[:, 0:16]
                sc.wait_ge(c.dv_b.h, B["dv_b"] + tau + 1)
                sc.activation(stt[:, 5:6], stt[:, 3:4], AF.Ln, bias=c.cst[:, 0:1],
                              scale=1.0)
                sc.activation(stt[:, 5:6], stt[:, 5:6], AF.Exp,
                              bias=0.0, scale=c.cst[:, 1:2])
                c.ac_b.inc(sc.drain(), 1)
                sc.wait_ge(c.dv_c.h, B["dv_c"] + tau + 1)
                if tau >= 2:
                    sc.wait_ge(c.yd_b.h, B["yd_b"] + 16 * (2 * tau - 2))
                sc.activation(z0t[tau % 2][:], ysb[tau % 2][:], AF.Identity,
                              bias=stt[:, 6:7], scale=stt[:, 5:6])
                c.ac_c.inc(sc.drain(), 1)
                sc.wait_ge(c.dv_e.h, B["dv_e"] + tau + 1)
                sc.activation(stt[:, 13:14], stt[:, 11:12], AF.Ln,
                              bias=c.cst[:, 0:1], scale=1.0)
                sc.activation(stt[:, 13:14], stt[:, 13:14], AF.Exp,
                              bias=0.0, scale=c.cst[:, 1:2])
                c.ac_d.inc(sc.drain(), 1)
                sc.wait_ge(c.dv_f.h, B["dv_f"] + tau + 1)
                sc.activation(z0t[tau % 2][:], x1t[tau % 2][:], AF.Identity,
                              bias=stt[:, 14:15], scale=stt[:, 13:14])
                c.ac_d.inc(sc.drain(), 1)

        @blk.sync
        def _(sy):
            for tau in range(NT):
                if tau >= 2:
                    sy.wait_ge(c.dv_d.h, B["dv_d"] + tau - 1)
                c.yd_a.inc(sy.dma_start(xt[tau % 2][:],
                                        c.x_in[128 * tau:128 * (tau + 1), :]), 16)
                sy.wait_ge(c.dv_d.h, B["dv_d"] + tau + 1)
                c.yd_b.inc(sy.dma_start(c.d_x1[128 * tau:128 * (tau + 1), :],
                                        x1t[tau % 2][:]), 16)
                sy.wait_ge(c.ac_d.h, B["ac_d"] + 2 * (tau + 1))
                c.yd_b.inc(sy.dma_start(c.d_z[0][128 * tau:128 * (tau + 1), :],
                                        z0t[tau % 2][:]), 16)

    if "dbg_x1" in c.dbg:
        with nc.Block("p3dbg") as blk:
            @blk.sync
            def _(sy):
                sy.wait_ge(c.yd_b.h, c.yd_b.n)
                c.yd_c.inc(sy.dma_start(c.dbg["dbg_x1"][:], c.d_x1[:]), 16)
                c.yd_c.inc(sy.dma_start(c.dbg["dbg_z0"][:], c.d_z[0][:]), 16)
                sy.wait_ge(c.yd_c.h, c.yd_c.n)

    st.close()
    c.p23_stack.close()


class Sched:

    def __init__(self, base):
        self.n = base
        self.marks = {}

    def tick(self, *key):
        self.n += 1
        self.marks[key] = self.n
        return self.n

    def at(self, *key):
        return self.marks[key]


def pa_inner_attn(c, li):
    nc = c.nc
    st = contextlib.ExitStack()

    def sb(name, shape, dt):
        return st.enter_context(nc.sbuf_tensor(f"{name}_{li}", shape, dt))

    def psb(name, shape, dt=F32):
        return st.enter_context(nc.psum_tensor(f"{name}_{li}", shape, dt))

    L = {}
    VB = 194
    L["Wq"] = sb("pa_wq", [128, 8 * 3 * D], BF16)
    L["Wo"] = sb("pa_wo", [128, 8 * D], BF16)
    L["CQ"] = sb("pa_cq", [128, S], BF16)
    L["SQ"] = sb("pa_sq", [128, S], BF16)
    L["CK"] = sb("pa_ck", [128, S], BF16)
    L["SK"] = sb("pa_sk", [128, S], BF16)
    L["qs_c"] = sb("pa_qsc", [128, 4], F32)
    L["fence"] = sb("pa_fence", [1, 128], BF16)
    L["trig"] = sb("pa_trig", [128, 2 * 512], F32)
    L["frs"] = sb("pa_frs", [128, 512], F32)
    L["nr"] = sb("pa_nr", [128, 512], F32)
    L["zt"] = [sb(f"pa_z{i}", [128, D], F32) for i in range(2)]
    L["zsh"] = [sb(f"pa_zs{i}", [128, 512], F32) for i in range(2)]
    L["tfm"] = sb("pa_tfm", [128, 8 * 512], BF16)
    L["qT"] = sb("pa_qT", [128, 8 * 512], BF16)
    L["kT"] = sb("pa_kT", [128, 8 * 768], BF16)
    L["VE"] = sb("pa_ve", [128, 6 * 8 * 128], BF16)
    L["sq2"] = sb("pa_sq2", [128, 2 * 512], BF16)
    L["rsq"] = sb("pa_rsq", [128, 2 * 512], BF16)
    L["tm1"] = sb("pa_tm1", [128, 512], BF16)
    L["tm2"] = sb("pa_tm2", [128, 512], BF16)
    L["pT"] = [sb(f"pa_pt{i}", [128, 1024], BF16) for i in range(2)]
    L["oT"] = sb("pa_oT", [128, 8 * 512], BF16)
    L["maxc"] = sb("pa_maxc", [128, 16], F32)
    L["max8"] = sb("pa_max8", [128, 16], F32)
    L["sums"] = sb("pa_sums", [128, 16], F32)
    L["lns"] = sb("pa_lns", [128, 16], F32)
    L["bcol"] = sb("pa_bcol", [128, 16], F32)
    L["bT"] = sb("pa_bT", [16, 128], BF16)
    L["psB"] = [psb(f"pa_psB{i}", [128, 1024]) for i in range(2)]
    L["oA"] = [psb(f"pa_oA{i}", [64, 512]) for i in range(2)]
    L["oB"] = [psb(f"pa_oB{i}", [128, 512]) for i in range(2)]
    L["vex"] = L["VE"][:].rearrange("p (k pb x) -> p k pb x", pb=8, x=128)
    L["wqv"] = L["Wq"][:].rearrange("p (dc f) -> p dc f", dc=8)
    L["wov"] = L["Wo"][:].rearrange("p (ec d) -> p ec d", ec=8)
    L["zin"], L["zout"] = c.d_z[0], c.d_z[1]

    with nc.Block(f"pa{li}t1") as blk:
        @blk.gpsimd
        def _(g):
            for col, srcd, rot in [(0, c.f_qs, 0), (1, c.f_qs, 1),
                                   (2, c.f_ks, 0), (3, c.f_ks, 1)]:
                for p0 in (0, 64):
                    if rot:
                        c.gd_a.inc(g.dma_start(
                            L["qs_c"][p0:p0 + 32, col:col + 1],
                            srcd[li][32:64, None]), 16)
                        c.gd_a.inc(g.dma_start(
                            L["qs_c"][p0 + 32:p0 + 64, col:col + 1],
                            srcd[li][0:32, None]), 16)
                    else:
                        c.gd_a.inc(g.dma_start(
                            L["qs_c"][p0:p0 + 64, col:col + 1],
                            srcd[li][:, None]), 16)
            g.memset(L["zsh"][0][0:1, :], 0.0)
            L["gd_qs"] = c.gd_a.n
            c.gd_a.inc(g.dma_start(L["wqv"],
                c.fWqkv[li].rearrange("(dc p) f -> p dc f", p=128)), 16)
            L["gd_wq"] = c.gd_a.n
            c.gd_a.inc(g.dma_start(L["wov"],
                c.fWo[li].rearrange("(ec p) d -> p ec d", p=128)), 16)
            L["gd_wo"] = c.gd_a.n

    for gi in range(8):
        B = c.bases()
        gsl = slice(512 * gi, 512 * (gi + 1))
        with nc.Block(f"pa{li}t2_{gi}") as blk:
            @blk.sync
            def _(sy):
                for dst, src in ((L["trig"][:, 0:512], c.r_cq),
                                 (L["trig"][:, 512:1024], c.r_sq),
                                 (L["frs"][:], c.r_ck),
                                 (L["nr"][:], c.r_sk)):
                    c.yd_c.inc(sy.dma_start(dst, src[:, gsl]), 16)
                c.yd_c.inc(sy.dma_start(L["fence"][0:1, :],
                                        c.d_QT[0:1, 0:128]), 16)

            @blk.vector
            def _(v):
                if gi == 0:
                    v.wait_ge(c.gd_a.h, L["gd_qs"])
                v.wait_ge(c.yd_c.h, c.yd_c.n)
                v.tensor_scalar(L["CQ"][:, gsl], L["trig"][:, 0:512],
                                L["qs_c"][:, 0:1], None, ALU.mult)
                v.tensor_scalar(L["SQ"][:, gsl], L["trig"][:, 512:1024],
                                L["qs_c"][:, 1:2], None, ALU.mult)
                v.tensor_scalar(L["CK"][:, gsl], L["frs"][:],
                                L["qs_c"][:, 2:3], None, ALU.mult)
                c.dv_a.inc(v.tensor_scalar(L["SK"][:, gsl], L["nr"][:],
                                           L["qs_c"][:, 3:4], None,
                                           ALU.mult), 1)

    for s_ in range(NSEG):
        _pa_segment(c, li, s_, L)

    if f"dbg_z1" in c.dbg and li == getattr(c, "dbg_pa_layer", 0):
        with nc.Block(f"pa{li}dbg") as blk:
            @blk.sync
            def _(sy):
                sy.wait_ge(c.yd_b.h, c.yd_b.n)
                c.yd_c.inc(sy.dma_start(c.dbg["dbg_z1"][:], c.d_z[1][:]), 16)
                sy.wait_ge(c.yd_c.h, c.yd_c.n)

    st.close()


def _pa_segment(c, li, s_, L):
    nc = c.nc
    VB = 194
    zt, zsh, tfm, qT, kT, VE = (L[k] for k in
        ["zt", "zsh", "tfm", "qT", "kT", "VE"])
    vex, wqv, wov = L["vex"], L["wqv"], L["wov"]
    CQ, SQ, CK, SK = L["CQ"], L["SQ"], L["CK"], L["SK"]
    sq2, rsq, tm1, tm2, pT, oT = (L[k] for k in
        ["sq2", "rsq", "tm1", "tm2", "pT", "oT"])
    maxc, max8, sums, lns, bcol, bT, nr = (L[k] for k in
        ["maxc", "max8", "sums", "lns", "bcol", "bT", "nr"])
    psB, oA, oB = L["psB"], L["oA"], L["oB"]
    zin, zout = L["zin"], L["zout"]
    frs = L["frs"]
    s0 = (s_ == 0)
    gsl = slice(512 * s_, 512 * (s_ + 1))

    def kcs_of(w):
        return [2, 3] if (s0 and w == 0) else [2 * w + k for k in range(4)]

    B = c.bases()
    PB = Sched(B["pe_b"])
    DD = Sched(B["dv_d"])
    PD = Sched(B["pe_d"])
    DF = Sched(B["dv_f"])
    ACA = Sched(B["ac_a"])
    ACD = Sched(B["ac_d"])
    DD.tick("halo")
    for ti in range(4):
        for hf in range(2):
            PB.tick("T", ti, hf)
            DD.tick("T", ti, hf)
    for fc in range(16):
        PB.tick("qkv", fc)
        DD.tick("qkv", fc)
    for ti in range(4):
        PB.tick("v", ti)
        DD.tick("v", ti)
    for side in ("q", "k"):
        for fc in range(8):
            DF.tick("sq2", side, fc)
            PD.tick("norm", side, fc)
        DF.tick("nmax", side)
        ACD.tick("rsq", side)
        DF.tick("rsq", side)
        for fc in range(8):
            PD.tick("bc", side, fc)
            PD.tick("rot", side, fc)
            DF.tick("app", side, fc)
    for w in range(2):
        for q_ in range(2):
            for pc in range(8):
                PD.tick("qk", w, q_, pc)
                DF.tick("max", w, q_, pc)
                ACD.tick("eqk", w, q_, pc)
                DF.tick("bias", w, q_, pc)
            PD.tick("bT", w, q_)
            DF.tick("bT", w, q_)
            for pc in range(10):
                if pc < 8:
                    PD.tick("sT", w, q_, pc)
                    ACA.tick("exp", w, q_, pc)
                if 1 <= pc <= 8:
                    PD.tick("pv", w, q_, pc - 1)
                if 2 <= pc:
                    DF.tick("oev", w, q_, pc - 2)
    for ti in range(4):
        PB.tick("wo", ti)
        DF.tick("res", ti)

    with nc.Block(f"pa{li}s{s_}") as blk:
        @blk.sync
        def _(sy):
            if s0:
                sy.wait_ge(c.yd_b.h, B["yd_b"])
            for ti in range(4):
                tau = 4 * s_ + ti
                if ti >= 2:
                    sy.wait_ge(c.pe_b.h, PB.at("T", ti - 2, 1))
                c.yd_a.inc(sy.dma_start(
                    zt[ti % 2][:], zin[128 * tau:128 * (tau + 1), :]), 16)
                if tau == 0:
                    c.yd_a.inc(sy.dma_start(zsh[0][1:128, :],
                                            zin[0:127, 512:1024]), 16)
                else:
                    c.yd_a.inc(sy.dma_start(
                        zsh[ti % 2][:],
                        zin[128 * tau - 1:128 * (tau + 1) - 1, 512:1024]), 16)
            c.yd_a.inc(sy.dma_start(L["fence"][0:1, :],
                                    c.d_QT[0:1, 0:128]), 16)
            for ti in range(4):
                tau = 4 * s_ + ti
                sy.wait_ge(c.pe_b.h, PB.at("T", 3, 1))
                if ti >= 2:
                    sy.wait_ge(c.yd_b.h, B["yd_b"] + 16 * (ti - 1))
                c.yd_a.inc(sy.dma_start(
                    zt[ti % 2][:], zin[128 * tau:128 * (tau + 1), :]), 16)
                c.yd_a.inc(sy.dma_start(L["fence"][0:1, :],
                                        c.d_QT[0:1, 0:128]), 16)
                sy.wait_ge(c.dv_f.h, DF.at("res", ti))
                c.yd_b.inc(sy.dma_start(zout[128 * tau:128 * (tau + 1), :],
                                        zt[ti % 2][:]), 16)

        @blk.tensor
        def _(t):
            for ti in range(4):
                nw = (2 * ti + 4) if ti < 3 else 9
                t.wait_ge(c.yd_a.h, B["yd_a"] + 16 * nw)
                for hf in range(2):
                    if 2 * ti + hf >= 2:
                        pt_, ph_ = divmod(2 * ti + hf - 2, 2)
                        t.wait_ge(c.dv_d.h, DD.at("T", pt_, ph_))
                    for k4 in range(4):
                        dc = 4 * hf + k4
                        srcb = (zt[ti % 2][:, 128 * dc:128 * (dc + 1)] if dc < 4
                                else zsh[ti % 2][:, 128 * (dc - 4):128 * (dc - 3)])
                        ins = nc.tensor.transpose(
                            oB[(2 * ti + hf) % 2][:, 128 * k4:128 * (k4 + 1)],
                            srcb, c.identf[:])
                    c.pe_b.inc(ins, 1)
            if s0:
                t.wait_ge(c.gd_a.h, L["gd_wq"])
            for fc in range(16):
                pq = psB[fc % 2]
                if fc < 2:
                    t.wait_ge(c.dv_d.h, DD.at("T", 3, 1))
                else:
                    t.wait_ge(c.dv_d.h, DD.at("qkv", fc - 2))
                if 8 <= fc < 10:
                    t.wait_ge(c.dv_d.h, DD.at("halo"))
                for dc in range(8):
                    ins = t.matmul(pq[:, 0:512],
                                   wqv[:, dc, 128 * fc:128 * (fc + 1)],
                                   tfm[:, 512 * dc:512 * (dc + 1)],
                                   start=(dc == 0), stop=(dc == 7))
                c.pe_b.inc(ins, 1)
            for ti in range(4):
                pv = psB[ti % 2]
                t.wait_ge(c.dv_d.h, DD.at("qkv", 14 + (ti % 2)))
                if ti >= 2:
                    t.wait_ge(c.dv_d.h, DD.at("v", ti - 2))
                for half in range(2):
                    for dc in range(8):
                        ins = t.matmul(
                            pv[:, 512 * half:512 * (half + 1)],
                            tfm[:, 512 * dc + 128 * ti:512 * dc + 128 * (ti + 1)],
                            wqv[:, dc, 2048 + 512 * half:2560 + 512 * half],
                            start=(dc == 0), stop=(dc == 7))
                c.pe_b.inc(ins, 1)
            for side, nm in [(0, "q"), (1, "k")]:
                if nm == "q":
                    t.wait_ge(c.dv_d.h, DD.at("T", 3, 1))
                for fc in range(8):
                    t.wait_ge(c.dv_f.h, DF.at("sq2", nm, fc))
                    c.pe_d.inc(t.matmul(
                        oB[side][:], c.nlut[:, 128 * fc:128 * (fc + 1)],
                        sq2[:, 512 * (fc % 2):512 * (fc % 2 + 1)],
                        start=(fc == 0), stop=(fc == 7)), 1)
                for fc in range(8):
                    t.wait_ge(c.dv_f.h, DF.at("rsq", nm))
                    if side == 0 and fc in (0, 1):
                        t.wait_ge(c.dv_f.h, DF.at("nmax", nm))
                    elif fc >= 2 or side == 1:
                        pnm, pfc = (nm, fc - 2) if fc >= 2 else ("q", 6 + fc)
                        t.wait_ge(c.dv_f.h, DF.at("app", pnm, pfc))
                    c.pe_d.inc(t.matmul(oB[fc % 2][:],
                                        c.bsel[:, 128 * fc:128 * (fc + 1)],
                                        rsq[:, 512 * side:512 * (side + 1)],
                                        start=True, stop=True), 1)
                    srcc = (qT[:, 512 * fc:512 * (fc + 1)] if side == 0 else
                            kT[:].rearrange("p (pb x) -> p pb x", pb=8)[
                                :, fc, 256:768])
                    c.pe_d.inc(t.matmul(psB[fc % 2][:, 0:512], c.Pmat[:], srcc,
                                        start=True, stop=True), 1)
            for w in range(2):
                kcs = kcs_of(w)
                nkc = len(kcs)
                klen = 128 * nkc
                for q_ in range(2):
                    for pc in range(8):
                        pq = psB[pc % 2]
                        if (w, q_) == (0, 0) and pc < 2:
                            t.wait_ge(c.dv_f.h, DF.at("app", "k", 6 + pc))
                        elif pc < 2:
                            pw, pq_ = (w, q_ - 1) if q_ else (w - 1, 1)
                            t.wait_ge(c.ac_a.h, ACA.at("exp", pw, pq_, 6 + pc))
                        else:
                            t.wait_ge(c.ac_d.h, ACD.at("eqk", w, q_, pc - 2))
                        mrhs = (c.Mqk0[:, 256 * q_:256 * (q_ + 1)] if (s0 and w == 0)
                                else c.Mqk[:, 512 * q_:512 * (q_ + 1)])
                        for hh in range(2):
                            co = 512 * hh
                            r0, r1 = (0, 64) if hh == 0 else (64, 128)
                            tp = (0, 0) if hh == 0 else (64, 0)
                            t.matmul(pq[:, co:co + klen], c.identb[:], mrhs,
                                     start=True, stop=False)
                            ins = t.matmul(
                                pq[:, co:co + klen],
                                qT[r0:r1, 512 * pc + 256 * w + 128 * q_:
                                   512 * pc + 256 * w + 128 * (q_ + 1)],
                                kT[r0:r1, 768 * pc + 128 * kcs[0]:
                                   768 * pc + 128 * kcs[0] + klen],
                                start=False, stop=True, tile_position=tp)
                        c.pe_d.inc(ins, 1)
                    t.wait_ge(c.dv_f.h, DF.at("bias", w, q_, 7))
                    if (w, q_) != (0, 0):
                        pw, pq_ = (w, q_ - 1) if q_ else (w - 1, 1)
                        t.wait_ge(c.dv_f.h, DF.at("oev", pw, pq_, 7))
                    c.pe_d.inc(nc.tensor.transpose(
                        oA[0][0:16, 128:256], bcol[:], c.identf[:]), 1)
                    for pc in range(10):
                        if pc < 8:
                            ps = psB[pc % 2]
                            t.wait_ge(c.dv_f.h, DF.at("bT", w, q_))
                            if pc >= 2:
                                t.wait_ge(c.ac_a.h, ACA.at("exp", w, q_, pc - 2))
                            else:
                                t.wait_ge(c.ac_d.h, ACD.at("eqk", w, q_, 6 + pc))
                            mt = (c.MT0[:, 256 * q_:256 * (q_ + 1)] if (s0 and w == 0)
                                  else c.MT[:, 512 * q_:512 * (q_ + 1)])
                            for hh in range(2):
                                co = 512 * hh
                                r0, r1 = (0, 64) if hh == 0 else (64, 128)
                                tp = (0, 0) if hh == 0 else (64, 0)
                                h = 2 * pc + hh
                                t.matmul(ps[:, co:co + klen], c.identb[:], mt,
                                         start=True, stop=False)
                                for ci in range(nkc):
                                    t.matmul(ps[:, co + 128 * ci:co + 128 * (ci + 1)],
                                             c.sel16[:, 128 * h:128 * (h + 1)],
                                             bT[0:16, 0:128],
                                             start=False, stop=False)
                                for ci, kc in enumerate(kcs):
                                    ins = t.matmul(
                                        ps[:, co + 128 * ci:co + 128 * (ci + 1)],
                                        kT[r0:r1, 768 * pc + 128 * kc:
                                           768 * pc + 128 * (kc + 1)],
                                        qT[r0:r1, 512 * pc + 256 * w + 128 * q_:
                                           512 * pc + 256 * w + 128 * (q_ + 1)],
                                        start=False, stop=(ci == nkc - 1),
                                        tile_position=tp)
                            c.pe_d.inc(ins, 1)
                        if 1 <= pc <= 8:
                            ppc = pc - 1
                            t.wait_ge(c.ac_a.h, ACA.at("exp", w, q_, ppc))
                            if (w, q_) == (0, 0) and ppc < 2:
                                t.wait_ge(c.dv_f.h, DF.at("app", "k", 6 + ppc))
                            elif ppc >= 2:
                                t.wait_ge(c.dv_f.h, DF.at("oev", w, q_, ppc - 2))
                            else:
                                pw, pq_ = (w, q_ - 1) if q_ else (w - 1, 1)
                                t.wait_ge(c.dv_f.h,
                                          DF.at("oev", pw, pq_, 6 + ppc))
                            for ci, kc in enumerate(kcs):
                                t.matmul(oA[ppc % 2][:, 0:128],
                                         vex[:, kc, ppc, 0:64],
                                         pT[ppc % 2][:, 128 * ci:128 * (ci + 1)],
                                         start=(ci == 0), stop=(ci == nkc - 1))
                                ins = t.matmul(
                                    oB[ppc % 2][:, 0:128],
                                    vex[:, kc, ppc, 0:128],
                                    pT[ppc % 2][:, 512 + 128 * ci:512 + 128 * (ci + 1)],
                                    start=(ci == 0), stop=(ci == nkc - 1))
                            c.pe_d.inc(ins, 1)
            if s0:
                t.wait_ge(c.gd_a.h, L["gd_wo"])
            for ti in range(4):
                pw_ = psB[ti % 2]
                t.wait_ge(c.dv_f.h, DF.at("oev", 1, 1, 7))
                if ti >= 2:
                    t.wait_ge(c.dv_f.h, DF.at("res", ti - 2))
                for half in range(2):
                    for ec in range(8):
                        ins = t.matmul(
                            pw_[:, 512 * half:512 * (half + 1)],
                            oT[:, 512 * ec + 128 * ti:512 * ec + 128 * (ti + 1)],
                            wov[:, ec, 512 * half:512 * (half + 1)],
                            start=(ec == 0), stop=(ec == 7))
                c.pe_b.inc(ins, 1)

        @blk.vector
        def _(v):
            if s0:
                c.dv_d.inc(v.tensor_copy(nr[:, 0:1], c.iota_c[:]), 1)
            else:
                v.tensor_copy(
                    kT[:].rearrange("p (pb x) -> p pb x", pb=8)[:, :, 0:256],
                    kT[:].rearrange("p (pb x) -> p pb x", pb=8)[:, :, 512:768])
                c.dv_d.inc(v.tensor_copy(VE[:, 0:2048], VE[:, 4096:6144]), 1)
            for ti in range(4):
                for hf in range(2):
                    v.wait_ge(c.pe_b.h, PB.at("T", ti, hf))
                    dst = tfm[:].rearrange("p (dc x) -> p dc x", dc=8)[
                        :, 4 * hf:4 * hf + 4, 128 * ti:128 * (ti + 1)]
                    c.dv_d.inc(v.tensor_copy(
                        dst, oB[(2 * ti + hf) % 2][:].rearrange(
                            "p (k x) -> p k x", k=4)), 1)
            for fc in range(16):
                v.wait_ge(c.pe_b.h, PB.at("qkv", fc))
                if fc < 8:
                    dst = qT[:, 512 * fc:512 * (fc + 1)]
                else:
                    dst = kT[:].rearrange("p (pb x) -> p pb x", pb=8)[
                        :, fc - 8, 256:768]
                c.dv_d.inc(v.tensor_copy(dst, psB[fc % 2][:, 0:512]), 1)
            for ti in range(4):
                v.wait_ge(c.pe_b.h, PB.at("v", ti))
                c.dv_d.inc(v.tensor_copy(
                    VE[:, 1024 * (2 + ti):1024 * (3 + ti)],
                    psB[ti % 2][:, 0:1024]), 1)
            for side, nm in [(0, "q"), (1, "k")]:
                for fc in range(8):
                    v.wait_ge(c.dv_d.h, DD.at("qkv", fc if side == 0 else 8 + fc))
                    if (side, fc) not in ((0, 0), (0, 1)):
                        pnm, pfc = (nm, fc - 2) if fc >= 2 else ("q", 6 + fc)
                        v.wait_ge(c.pe_d.h, PD.at("norm", pnm, pfc))
                    if side == 0:
                        srcq = qT[:, 512 * fc:512 * (fc + 1)]
                    else:
                        srcq = kT[:].rearrange("p (pb x) -> p pb x", pb=8)[
                            :, fc, 256:768]
                    c.dv_f.inc(v.tensor_tensor(
                        sq2[:, 512 * (fc % 2):512 * (fc % 2 + 1)],
                        srcq, srcq, ALU.mult), 1)
                v.wait_ge(c.pe_d.h, PD.at("norm", nm, 7))
                c.dv_f.inc(v.tensor_scalar(nr[:], oB[side][:], 1e-30, None,
                                           ALU.max), 1)
                v.wait_ge(c.ac_d.h, ACD.at("rsq", nm))
                v.tensor_scalar(frs[:], frs[:], 1e-12, None, ALU.max)
                with nc.allow_low_precision(reason="rsq feeds bf16 matmul"):
                    c.dv_f.inc(v.reciprocal(rsq[:, 512 * side:512 * (side + 1)],
                                            frs[:]), 1)
                TBL = (CQ, SQ) if side == 0 else (CK, SK)
                for fc in range(8):
                    v.wait_ge(c.pe_d.h, PD.at("rot", nm, fc))
                    if side == 0:
                        srcc = qT[:, 512 * fc:512 * (fc + 1)]
                    else:
                        srcc = kT[:].rearrange("p (pb x) -> p pb x", pb=8)[
                            :, fc, 256:768]
                    v.tensor_tensor(tm1[:], srcc, TBL[0][:, gsl], ALU.mult)
                    v.tensor_tensor(tm2[:], psB[fc % 2][:, 0:512],
                                    TBL[1][:, gsl], ALU.mult)
                    v.tensor_tensor(tm1[:], tm1[:], tm2[:], ALU.add)
                    c.dv_f.inc(v.tensor_tensor(srcc, tm1[:], oB[fc % 2][:],
                                               ALU.mult), 1)
            for w in range(2):
                kcs = kcs_of(w)
                klen = 128 * len(kcs)
                for q_ in range(2):
                    for pc in range(8):
                        v.wait_ge(c.pe_d.h, PD.at("qk", w, q_, pc))
                        gv = psB[pc % 2][:].rearrange("p (g x) -> p g x", g=2)
                        v.tensor_reduce(maxc[:, 2 * pc:2 * pc + 2],
                                        gv[:, :, 0:klen], AX.X, ALU.max,
                                        negate=True)
                        v.tensor_copy(nr[:, 0:1], c.iota_c[:])
                        c.dv_f.inc(v.tensor_scalar(
                            max8[:, 2 * pc:2 * pc + 2],
                            maxc[:, 2 * pc:2 * pc + 2], 8.0, None, ALU.mult), 1)
                        v.wait_ge(c.ac_d.h, ACD.at("eqk", w, q_, pc))
                        c.dv_f.inc(v.scalar_tensor_tensor(
                            bcol[:, 2 * pc:2 * pc + 2],
                            lns[:, 2 * pc:2 * pc + 2], -0.125,
                            maxc[:, 2 * pc:2 * pc + 2], ALU.mult, ALU.add), 1)
                    v.wait_ge(c.pe_d.h, PD.at("bT", w, q_))
                    c.dv_f.inc(v.tensor_copy(bT[0:16, 0:128],
                                             oA[0][0:16, 128:256]), 1)
                    for pc in range(8):
                        v.wait_ge(c.pe_d.h, PD.at("pv", w, q_, pc))
                        od = oT[:, 512 * pc + 256 * w + 128 * q_:
                                512 * pc + 256 * w + 128 * (q_ + 1)]
                        v.tensor_copy(od[0:64, :], oA[pc % 2][0:64, 0:128])
                        c.dv_f.inc(v.tensor_copy(od[64:128, :],
                                                 oB[pc % 2][64:128, 0:128]), 1)
            for ti in range(4):
                v.wait_ge(c.pe_b.h, PB.at("wo", ti))
                v.wait_ge(c.yd_a.h, B["yd_a"] + 16 * (11 + 2 * ti))
                c.dv_f.inc(v.tensor_tensor(zt[ti % 2][:], zt[ti % 2][:],
                                           psB[ti % 2][:], ALU.add), 1)

        @blk.scalar
        def _(sc):
            for side, nm in [(0, "q"), (1, "k")]:
                sc.wait_ge(c.dv_f.h, DF.at("nmax", nm))
                sc.activation(frs[:], nr[:], AF.Ln, bias=0.0, scale=1.0)
                sc.activation(frs[:], frs[:], AF.Exp, bias=0.0,
                              scale=c.cst[:, 10:11])
                c.ac_d.inc(sc.drain(), 1)
            for w in range(2):
                kcs = kcs_of(w)
                klen = 128 * len(kcs)
                for q_ in range(2):
                    for pc in range(8):
                        sc.wait_ge(c.dv_f.h, DF.at("max", w, q_, pc))
                        for hh in range(2):
                            h = 2 * pc + hh
                            sc.activation(
                                (tm1 if hh == 0 else tm2)[:, 0:klen],
                                psB[pc % 2][:, 512 * hh:512 * hh + klen],
                                AF.Exp, bias=max8[:, h:h + 1],
                                scale=c.cst[:, 4:5],
                                accum_out=sums[:, h:h + 1])
                        sc.activation(lns[:, 2 * pc:2 * pc + 2],
                                      sums[:, 2 * pc:2 * pc + 2], AF.Ln,
                                      bias=0.0, scale=1.0)
                        c.ac_d.inc(sc.drain(), 1)
                    for pc in range(8):
                        sc.wait_ge(c.pe_d.h, PD.at("sT", w, q_, pc))
                        if pc >= 2:
                            sc.wait_ge(c.pe_d.h, PD.at("pv", w, q_, pc - 2))
                        pv = pT[pc % 2][:].rearrange("p (g x) -> p g x", g=2)
                        gv = psB[pc % 2][:].rearrange("p (g x) -> p g x", g=2)
                        sc.activation(pv[:, :, 0:klen], gv[:, :, 0:klen],
                                      AF.Exp, bias=0.0, scale=c.cst[:, 4:5])
                        c.ac_a.inc(sc.drain(), 1)

    return


def _prev_iter(w, q_, pc):
    i = ((w * 2 + q_) * 8 + pc) - 2
    return i // 16, (i // 8) % 2, i % 8


def pb_inner_ffn(c, li, final=False):
    nc = c.nc
    st = contextlib.ExitStack()

    def sb(name, shape, dt):
        return st.enter_context(nc.sbuf_tensor(f"{name}_{li}", shape, dt))

    def psb(name, shape, dt=F32):
        return st.enter_context(nc.psum_tensor(f"{name}_{li}", shape, dt))

    W1 = sb("pb_w1", [128, 8 * 2 * DI], BF16)
    W2 = sb("pb_w2", [128, NJC * D], BF16)
    b2r = sb("pb_b2", [1, D], BF16)
    gcol = sb("pb_gc", [128, 8], F32)
    bcol = sb("pb_bc", [128, 8], F32)
    b1c = sb("pb_b1c", [128, 44], F32)

    fence = sb("pb_fence", [1, 128], BF16)
    zt = [sb(f"pb_z{i}", [128, D], F32) for i in range(2)]
    zsh = [sb(f"pb_zs{i}", [128, 512], F32) for i in range(2)]
    L_zsh = zsh
    xn = sb("pb_xn", [128, D], F32)
    xfm = sb("pb_xfm", [128, 8 * 512], BF16)
    hsb = sb("pb_h", [128, NJC * 512], BF16)
    gg = sb("pb_gg", [128, 512], BF16)
    stat = sb("pb_stat", [128, 16], F32)
    sq = sb("pb_sq", [128, D], F32)
    zo2 = [sq, sq] if final else None
    x1t = [xn, xn] if final else None
    W1v = W1[:].rearrange("p (dc f) -> p dc f", dc=8)

    psB = [psb(f"pb_psB{i}", [128, 1024]) for i in range(2)]
    pw2 = [psb(f"pb_pw2{i}", [128, 1024]) for i in range(2)]

    zin, zout = c.d_z[1], c.d_z[0]

    B = c.bases()
    rstage = sb("pb_rst", [128, 128], BF16)
    gdc = {}
    with nc.Block(f"pb{li}w") as blk:
        @blk.gpsimd
        def _(g):
            c.gd_a.inc(g.dma_start(c.d_rows[:].rearrange("a b c -> (a b c)"),
                                   c.zero_rows[:]), 16)
            g.wait_ge(c.gd_a.h, c.gd_a.n)
            c.gd_a.inc(g.dma_start(
                c.d_rows[0].rearrange("a b -> (a b)")[0:2730],
                c.f_b1[li][0:2730]), 16)
            c.gd_a.inc(g.dma_start(
                c.d_rows[1].rearrange("a b -> (a b)")[0:2730],
                c.f_b1[li][2730:5460]), 16)
            c.gd_a.inc(g.dma_start(
                c.d_rows[2].rearrange("a b -> (a b)")[0:1024],
                c.f_lng[li][:]), 16)
            c.gd_a.inc(g.dma_start(
                c.d_rows[3].rearrange("a b -> (a b)")[0:1024],
                c.f_lnb[li][:]), 16)
            gdc["rows"] = c.gd_a.n
            g.memset(zsh[0][0:1, :], 0.0)
            c.gd_a.inc(g.dma_start(b2r[0:1, :], c.f_b2[li][None, :]), 16)
            c.gd_a.inc(g.dma_start(
                W1[:].rearrange("p (dc f) -> p dc f", dc=8),
                c.fW1[li].rearrange("(dc p) f -> p dc f", p=128)), 16)
            gdc["w1"] = c.gd_a.n
            c.gd_a.inc(g.dma_start(
                W2[:].rearrange("p (jc d) -> p jc d", jc=NJC)[:, 0:21, :],
                c.fW2[li][0:2688].rearrange("(jc p) d -> p jc d", p=128)), 16)
            c.gd_a.inc(g.dma_start(
                W2[0:JLAST, 21 * D:22 * D], c.fW2[li][2688:2730, :]), 16)
            gdc["w2"] = c.gd_a.n

        @blk.sync
        def _(sy):
            sy.wait_ge(c.gd_a.h, gdc["rows"])
            for i in range(4):
                c.yd_c.inc(sy.dma_start(rstage[:, 32 * i:32 * (i + 1)],
                                        c.d_rows[i], transpose=True), 16)
            c.yd_c.inc(sy.dma_start(fence[0:1, :], c.d_QT[0:1, 0:128]), 16)

        @blk.vector
        def _(v):
            v.wait_ge(c.yd_c.h, c.yd_c.n)
            v.tensor_copy(b1c[:, 0:22], rstage[:, 0:22])
            v.tensor_copy(b1c[:, 22:44], rstage[:, 32:54])
            v.tensor_copy(gcol[:], rstage[:, 64:72])
            c.dv_a.inc(v.tensor_copy(bcol[:], rstage[:, 96:104]), 1)

    for s_ in range(NSEG):
        _pb_segment(c, li, s_, final, locals())

    if "dbg_z2" in c.dbg and not final:
        with nc.Block(f"pb{li}dbg") as blk:
            @blk.sync
            def _(sy):
                sy.wait_ge(c.yd_b.h, c.yd_b.n)
                c.yd_c.inc(sy.dma_start(c.dbg["dbg_z2"][:], c.d_z[0][:]), 16)
                sy.wait_ge(c.yd_c.h, c.yd_c.n)

    st.close()


def _pb_segment(c, li, s_, final, L):
    nc = c.nc
    zt, zsh, xn, xfm, hsb, gg, stat, sq = (L[k] for k in
        ["zt", "zsh", "xn", "xfm", "hsb", "gg", "stat", "sq"])
    W1v, W2, b2r, gcol, bcol, b1c = (L[k] for k in
        ["W1v", "W2", "b2r", "gcol", "bcol", "b1c"])
    psB, pw2 = L["psB"], L["pw2"]
    zin, zout = L["zin"], L["zout"]
    zo2, x1t = L["zo2"], L["x1t"]
    fence = L["fence"]
    gdc = L["gdc"]
    s0 = (s_ == 0)

    B = c.bases()
    PB = Sched(B["pe_b"])
    PD = Sched(B["pe_d"])
    DD = Sched(B["dv_d"])
    DF = Sched(B["dv_f"])
    ACA = Sched(B["ac_a"])
    ACD = Sched(B["ac_d"])
    for ti in range(4):
        DF.tick("st1", ti)
        ACD.tick("rstd", ti)
        DF.tick("bias", ti)
        ACD.tick("apply", ti)
        for hf in range(2):
            PB.tick("T", ti, hf)
        if ti >= 1:
            for hf in range(2):
                DD.tick("T", ti - 1, hf)
    for hf in range(2):
        DD.tick("T", 3, hf)
    for jc in range(NJC):
        PD.tick("w1", jc)
        ACA.tick("gelu", jc)
        DF.tick("h", jc)
    for ti in range(4):
        PB.tick("w2", ti)
        DF.tick("res", ti)
        if final:
            DF.tick("st2", ti)
            ACD.tick("rstd2", ti)
            DF.tick("bias2", ti)
            ACD.tick("yn", ti)
            DF.tick("out", ti)

    with nc.Block(f"pb{li}s{s_}") as blk:
        if s_ == 0 and li == 0 and "dbg_pbx" in c.dbg:
            @blk.gpsimd
            def _(g):
                g.wait_ge(c.ac_d.h, ACD.at("apply", 0))
                dbx = c.dbg["dbg_pbx"]
                c.gd_b.inc(g.dma_start(dbx[384:512, 0:16], L["stat"][:]), 16)
                c.gd_b.inc(g.dma_start(dbx[256:384, :], L["xn"][:]), 16)
                c.gd_b.inc(g.dma_start(dbx[128:256, 0:512], L["zsh"][0][:]), 16)
                c.gd_b.inc(g.dma_start(dbx[0:128, :], L["zt"][0][:]), 16)
                g.wait_ge(c.gd_b.h, c.gd_b.n)

        @blk.sync
        def _(sy):
            if s0:
                sy.wait_ge(c.yd_b.h, B["yd_b"])
            for ti in range(4):
                tau = 4 * s_ + ti
                if ti >= 2:
                    sy.wait_ge(c.ac_d.h, ACD.at("apply", ti - 2))
                c.yd_a.inc(sy.dma_start(
                    zt[ti % 2][:], zin[128 * tau:128 * (tau + 1), :]), 16)
                if tau == 0:
                    c.yd_a.inc(sy.dma_start(zsh[0][1:128, :],
                                            zin[0:127, 512:1024]), 16)
                else:
                    c.yd_a.inc(sy.dma_start(
                        zsh[ti % 2][:],
                        zin[128 * tau - 1:128 * (tau + 1) - 1, 512:1024]), 16)
            c.yd_a.inc(sy.dma_start(fence[0:1, :], c.d_QT[0:1, 0:128]), 16)
            for ti in range(4):
                tau = 4 * s_ + ti
                sy.wait_ge(c.pe_b.h, PB.at("T", 3, 1))
                if ti >= 2:
                    sy.wait_ge(c.yd_b.h, B["yd_b"] + 16 * (ti - 1))
                c.yd_a.inc(sy.dma_start(
                    zt[ti % 2][:], zin[128 * tau:128 * (tau + 1), :]), 16)
                if final:
                    c.yd_a.inc(sy.dma_start(
                        x1t[ti % 2][:], c.d_x1[128 * tau:128 * (tau + 1), :]), 16)
                c.yd_a.inc(sy.dma_start(fence[0:1, :],
                                        c.d_QT[0:1, 0:128]), 16)
                if final:
                    sy.wait_ge(c.dv_f.h, DF.at("out", ti))
                    c.yd_b.inc(sy.dma_start(
                        c.out[128 * tau:128 * (tau + 1), :], zo2[ti % 2][:]), 16)
                else:
                    sy.wait_ge(c.dv_f.h, DF.at("res", ti))
                    c.yd_b.inc(sy.dma_start(
                        zout[128 * tau:128 * (tau + 1), :], zt[ti % 2][:]), 16)

        @blk.vector
        def _(v):
            def _evacs(ti):
                for hf in range(2):
                    v.wait_ge(c.pe_b.h, PB.at("T", ti, hf))
                    for k4 in range(4):
                        dc = 4 * hf + k4
                        ins = v.tensor_scalar(
                            xfm[:, 512 * dc + 128 * ti:512 * dc + 128 * (ti + 1)],
                            pw2[(2 * ti + hf) % 2][:, 128 * k4:128 * (k4 + 1)],
                            gcol[:, dc:dc + 1], bcol[:, dc:dc + 1],
                            ALU.mult, ALU.add)
                    c.dv_d.inc(ins, 1)
            for ti in range(4):
                nw = (2 * ti + 4) if ti < 3 else 9
                v.wait_ge(c.yd_a.h, B["yd_a"] + 16 * nw)
                if final and ti == 0:
                    v.wait_ge(c.yd_b.h, B["yd_b"])
                zl = zt[ti % 2][:, 0:512]
                zh = zsh[ti % 2][:]
                v.tensor_reduce(stat[:, 0:1], zl, AX.X, ALU.add)
                v.tensor_reduce(stat[:, 1:2], zh, AX.X, ALU.add)
                v.scalar_tensor_tensor(sq[:, 0:512], zl, 1.0, zl, ALU.mult,
                                       ALU.mult, accum_out=stat[:, 2:3])
                v.tensor_tensor(stat[:, 0:1], stat[:, 0:1], stat[:, 1:2], ALU.add)
                v.scalar_tensor_tensor(sq[:, 0:512], zh, 1.0, zh, ALU.mult,
                                       ALU.mult, accum_out=stat[:, 3:4])
                v.tensor_scalar(stat[:, 4:5], stat[:, 0:1], 1.0 / D, None, ALU.mult)
                v.tensor_tensor(stat[:, 2:3], stat[:, 2:3], stat[:, 3:4], ALU.add)
                v.tensor_tensor(stat[:, 6:7], stat[:, 4:5], stat[:, 4:5], ALU.mult)
                v.tensor_scalar(stat[:, 5:6], stat[:, 2:3], 1.0 / D, None, ALU.mult)
                v.tensor_copy(sq[:, 0:1], c.iota_c[:])
                c.dv_f.inc(v.tensor_tensor(stat[:, 5:6], stat[:, 5:6],
                                           stat[:, 6:7], ALU.subtract), 1)
                v.wait_ge(c.ac_d.h, ACD.at("rstd", ti))
                c.dv_f.inc(v.tensor_scalar(stat[:, 8:9], stat[:, 4:5],
                                           stat[:, 7:8], -1.0, ALU.mult,
                                           ALU.mult), 1)
                if ti >= 1:
                    _evacs(ti - 1)
            _evacs(3)
            for jc in range(NJC):
                klen = JLAST if jc == NJC - 1 else 128
                v.wait_ge(c.ac_a.h, ACA.at("gelu", jc))
                c.dv_f.inc(v.scalar_tensor_tensor(
                    hsb[0:klen, 512 * jc:512 * (jc + 1)],
                    psB[jc % 2][0:klen, 0:512], b1c[0:klen, jc:jc + 1],
                    gg[0:klen, :], ALU.add, ALU.mult), 1)
            for ti in range(4):
                v.wait_ge(c.pe_b.h, PB.at("w2", ti))
                kit = 3 if final else 2
                v.wait_ge(c.yd_a.h, B["yd_a"] + 16 * (9 + kit * (ti + 1)))
                c.dv_f.inc(v.tensor_tensor(zt[ti % 2][:], zt[ti % 2][:],
                                           pw2[ti % 2][:], ALU.add), 1)
                if final:
                    zr = zt[ti % 2]
                    v.wait_ge(c.yd_b.h, B["yd_b"] + 16 * ti)
                    v.tensor_reduce(stat[:, 0:1], zr[:], AX.X, ALU.add)
                    v.scalar_tensor_tensor(sq[:], zr[:], 1.0, zr[:], ALU.mult,
                                           ALU.mult, accum_out=stat[:, 2:3])
                    v.tensor_scalar(stat[:, 4:5], stat[:, 0:1], 1.0 / D, None,
                                    ALU.mult)
                    v.tensor_scalar(stat[:, 5:6], stat[:, 2:3], 1.0 / D, None,
                                    ALU.mult)
                    v.tensor_tensor(stat[:, 6:7], stat[:, 4:5], stat[:, 4:5],
                                    ALU.mult)
                    v.tensor_copy(sq[:, 1:2], c.iota_c[:])
                    c.dv_f.inc(v.tensor_tensor(stat[:, 5:6], stat[:, 5:6],
                                               stat[:, 6:7], ALU.subtract), 1)
                    v.wait_ge(c.ac_d.h, ACD.at("rstd2", ti))
                    c.dv_f.inc(v.tensor_scalar(stat[:, 8:9], stat[:, 4:5],
                                               stat[:, 7:8], -1.0, ALU.mult,
                                               ALU.mult), 1)
                    v.wait_ge(c.ac_d.h, ACD.at("yn", ti))
                    v.wait_ge(c.yd_a.h, B["yd_a"] + 16 * (9 + 3 * (ti + 1)))
                    c.dv_f.inc(v.tensor_tensor(zo2[ti % 2][:], zo2[ti % 2][:],
                                               x1t[ti % 2][:], ALU.add), 1)

        @blk.scalar
        def _(sc):
            for ti in range(4):
                sc.wait_ge(c.dv_f.h, DF.at("st1", ti))
                sc.activation(stat[:, 7:8], stat[:, 5:6], AF.Ln,
                              bias=c.cst[:, 0:1], scale=1.0)
                sc.activation(stat[:, 7:8], stat[:, 7:8], AF.Exp, bias=0.0,
                              scale=c.cst[:, 1:2])
                c.ac_d.inc(sc.drain(), 1)
                sc.wait_ge(c.dv_f.h, DF.at("bias", ti))
                if ti >= 1:
                    sc.wait_ge(c.pe_b.h, PB.at("T", ti - 1, 1))
                sc.activation(xn[:, 0:512], zt[ti % 2][:, 0:512], AF.Identity,
                              bias=stat[:, 8:9], scale=stat[:, 7:8])
                sc.activation(xn[:, 512:1024], zsh[ti % 2][:], AF.Identity,
                              bias=stat[:, 8:9], scale=stat[:, 7:8])
                c.ac_d.inc(sc.drain(), 1)
            for jc in range(NJC):
                klen = JLAST if jc == NJC - 1 else 128
                sc.wait_ge(c.pe_d.h, PD.at("w1", jc))
                if jc >= 1:
                    sc.wait_ge(c.dv_f.h, DF.at("h", jc - 1))
                sc.activation(gg[0:klen, :], psB[jc % 2][0:klen, 512:1024],
                              AF.Gelu, bias=b1c[0:klen, 22 + jc:23 + jc],
                              scale=1.0)
                c.ac_a.inc(sc.drain(), 1)
            if final:
                for ti in range(4):
                    sc.wait_ge(c.dv_f.h, DF.at("st2", ti))
                    sc.activation(stat[:, 7:8], stat[:, 5:6], AF.Ln,
                                  bias=c.cst[:, 0:1], scale=1.0)
                    sc.activation(stat[:, 7:8], stat[:, 7:8], AF.Exp, bias=0.0,
                                  scale=c.cst[:, 1:2])
                    c.ac_d.inc(sc.drain(), 1)
                    sc.wait_ge(c.dv_f.h, DF.at("bias2", ti))
                    sc.wait_ge(c.yd_b.h, B["yd_b"] + 16 * ti)
                    sc.activation(zo2[ti % 2][:], zt[ti % 2][:], AF.Identity,
                                  bias=stat[:, 8:9], scale=stat[:, 7:8])
                    c.ac_d.inc(sc.drain(), 1)

        @blk.tensor
        def _(t):
            for ti in range(4):
                t.wait_ge(c.ac_d.h, ACD.at("apply", ti))
                for hf in range(2):
                    if 2 * ti + hf >= 2:
                        pt_, ph_ = divmod(2 * ti + hf - 2, 2)
                        t.wait_ge(c.dv_d.h, DD.at("T", pt_, ph_))
                    for k4 in range(4):
                        dc = 4 * hf + k4
                        ins = nc.tensor.transpose(
                            pw2[(2 * ti + hf) % 2][:, 128 * k4:128 * (k4 + 1)],
                            xn[:, 128 * dc:128 * (dc + 1)], c.identf[:])
                    c.pe_b.inc(ins, 1)
            if s0:
                t.wait_ge(c.gd_a.h, gdc["w1"])
            for jc in range(NJC):
                klen = JLAST if jc == NJC - 1 else 128
                t.wait_ge(c.dv_d.h, DD.at("T", 3, 1))
                if jc >= 2:
                    t.wait_ge(c.dv_f.h, DF.at("h", jc - 2))
                pq = psB[jc % 2]
                for part, c0 in ((0, 128 * jc), (512, DI + 128 * jc)):
                    for dc in range(8):
                        ins = t.matmul(pq[0:klen, part:part + 512],
                                       W1v[:, dc, c0:c0 + klen],
                                       xfm[:, 512 * dc:512 * (dc + 1)],
                                       start=(dc == 0), stop=(dc == 7))
                c.pe_d.inc(ins, 1)
            if s0:
                t.wait_ge(c.gd_a.h, gdc["w2"])
            for ti in range(4):
                t.wait_ge(c.dv_d.h, DD.at("T", 3, 1))
                if True:
                    pass
                pw = pw2[ti % 2]
                if ti < 2:
                    t.wait_ge(c.dv_d.h, DD.at("T", 3, 1))
                else:
                    t.wait_ge(c.dv_f.h, DF.at("res", ti - 2))
                for half in range(2):
                    for jc in range(NJC):
                        klen = JLAST if jc == NJC - 1 else 128
                        t.wait_ge(c.dv_f.h, DF.at("h", jc))
                        t.matmul(pw[:, 512 * half:512 * (half + 1)],
                                 hsb[0:klen, 512 * jc + 128 * ti:
                                     512 * jc + 128 * (ti + 1)],
                                 W2[0:klen, D * jc + 512 * half:
                                    D * jc + 512 * half + 512],
                                 start=(jc == 0), stop=False)
                    ins = t.matmul(pw[:, 512 * half:512 * (half + 1)],
                                   c.ones_row[0:1, 0:128],
                                   b2r[0:1, 512 * half:512 * half + 512],
                                   start=False, stop=True)
                c.pe_b.inc(ins, 1)

    return


def _final_wait(c):
    nc = c.nc
    with nc.Block("fin") as blk:
        @blk.sync
        def _(sy):
            sy.wait_ge(c.yd_a.h, c.yd_a.n)
            sy.wait_ge(c.yd_b.h, c.yd_b.n)
            sy.wait_ge(c.yd_c.h, c.yd_c.n)
            sy.wait_ge(c.gd_a.h, c.gd_a.n)


def _in_map(inputs, core):
    m = {
        "x": np.ascontiguousarray(inputs["L0_attn_x"][core]),
        "W_qkv": inputs["L0_attn_W_qkv"],
        "b_qkv": inputs["L0_attn_b_qkv"],
        "W_o": inputs["L0_attn_W_o"],
        "b_o": inputs["L0_attn_b_o"],
        "fWqkv": inputs["L0_ffn_Wqkv"],
        "f_qs": inputs["L0_ffn_q_scale"],
        "f_ks": inputs["L0_ffn_k_scale"],
        "fWo": inputs["L0_ffn_Wo"],
        "f_lng": inputs["L0_ffn_ln_g"],
        "f_lnb": inputs["L0_ffn_ln_b"],
        "fW1": inputs["L0_ffn_W1"],
        "f_b1": inputs["L0_ffn_b1"],
        "fW2": inputs["L0_ffn_W2"],
        "zero_rows": np.zeros(4 * 32 * 128, dtype=np.float32),
        "f_b2": inputs["L0_ffn_b2"],
    }
    m.update(_rope_tables())
    return {k: np.ascontiguousarray(np.asarray(v, dtype=np.float32)) for k, v in m.items()}


_ROPE_CACHE = {}


def _rope_tables():
    if _ROPE_CACHE:
        return _ROPE_CACHE
    half = np.arange(0, HD, 2, dtype=np.float64)
    inv_freq = 1.0 / 10000.0 ** (half / HD)
    t = np.arange(S, dtype=np.float64)
    freqs = t[:, None] * inv_freq[None, :]
    base_scale = (half + 0.4 * HD) / (1.4 * HD)
    power = (t - S // 2) / 128.0
    scale = base_scale[None, :] ** power[:, None]
    cos, sin = np.cos(freqs), np.sin(freqs)
    e = np.arange(128) % HD
    e32 = e % 32
    sgn = np.where(e < 32, -1.0, 1.0)
    _ROPE_CACHE.update({
        "r_cq": (cos * scale).T[e32],
        "r_sq": (sin * scale).T[e32] * sgn[:, None],
        "r_ck": (cos / scale).T[e32],
        "r_sk": (sin / scale).T[e32] * sgn[:, None],
    })
    return _ROPE_CACHE


def kernel(**inputs):
    nc, c = build_nc()
    in_maps = [_in_map(inputs, core) for core in range(8)]
    res = run_bass_kernel_spmd(nc, in_maps, core_ids=list(range(8)))
    return np.stack([res.results[i]["out"] for i in range(8)], axis=0)


# version 07
# speedup vs baseline: 1.0155x; 1.0155x over previous
import contextlib
import numpy as np

import concourse.bass as bass
import concourse.mybir as mybir
from concourse.bass_utils import run_bass_kernel_spmd

F32 = mybir.dt.float32
BF16 = mybir.dt.bfloat16
AF = mybir.ActivationFunctionType
ALU = mybir.AluOpType
AX = mybir.AxisListType

S = 4096
D = 1024
NH = 16
HD = 64
WIN = 256
DI = 2730
DEPTH = 6
NEG = -1.0e30
OUTER_C = 12.0
BIG = 1.0e9
NT = S // 128
NSEG = 8
SEG = 512
NJC = (DI + 127) // 128
JLAST = DI - 21 * 128
LN_EPS = 1e-5

SEM_NAMES = ["pe_a", "pe_b", "pe_c", "pe_d",
             "dv_a", "dv_b", "dv_c", "dv_d", "dv_e", "dv_f",
             "ac_a", "ac_b", "ac_c", "ac_d",
             "po_a", "po_b", "gd_a", "gd_b", "yd_a", "yd_b", "yd_c"]


class Sem:
    def __init__(self, handle):
        self.h = handle
        self.n = 0

    def inc(self, instr, k=1):
        instr.then_inc(self.h, k)
        self.n += k
        return self.n


class Ctx:
    def bases(self):
        return {s: getattr(self, s).n for s in SEM_NAMES}


def _band(v, dst, n, lo, hi, c):
    a = c.mskA[:, 0:n]
    b = c.mskB[:, 0:n]
    v.tensor_scalar(a, c.iota_r[:, 0:n], c.iota_c[:], 0.0, ALU.subtract, ALU.add)
    v.tensor_scalar(b, a, float(lo), NEG, ALU.is_lt, ALU.mult)
    v.tensor_scalar(a, a, float(hi), NEG, ALU.is_gt, ALU.mult)
    v.tensor_tensor(dst, a, b, ALU.add)


def build_nc(debug=(), phases=("p1", "p2", "p3", "inner")):
    nc = bass.Bass("TRN2", target_bir_lowering=False,
                   detect_race_conditions=False)
    es = contextlib.ExitStack()
    c = Ctx()
    c.nc = nc
    c.es = es

    def sb(name, shape, dt):
        return es.enter_context(nc.sbuf_tensor(name, shape, dt))

    c.sb = sb

    c.x_in = nc.dram_tensor("x", [S, D], F32, kind="ExternalInput").ap()
    c.W_qkv = nc.dram_tensor("W_qkv", [3 * D, D], F32, kind="ExternalInput").ap()
    c.b_qkv = nc.dram_tensor("b_qkv", [3 * D], F32, kind="ExternalInput").ap()
    c.W_o = nc.dram_tensor("W_o", [D, D], F32, kind="ExternalInput").ap()
    c.b_o = nc.dram_tensor("b_o", [D], F32, kind="ExternalInput").ap()
    c.fWqkv = nc.dram_tensor("fWqkv", [DEPTH, D, 3 * D], F32, kind="ExternalInput").ap()
    c.f_qs = nc.dram_tensor("f_qs", [DEPTH, HD], F32, kind="ExternalInput").ap()
    c.f_ks = nc.dram_tensor("f_ks", [DEPTH, HD], F32, kind="ExternalInput").ap()
    c.fWo = nc.dram_tensor("fWo", [DEPTH, D, D], F32, kind="ExternalInput").ap()
    c.f_lng = nc.dram_tensor("f_lng", [DEPTH, D], F32, kind="ExternalInput").ap()
    c.f_lnb = nc.dram_tensor("f_lnb", [DEPTH, D], F32, kind="ExternalInput").ap()
    c.fW1 = nc.dram_tensor("fW1", [DEPTH, D, 2 * DI], F32, kind="ExternalInput").ap()
    c.f_b1 = nc.dram_tensor("f_b1", [DEPTH, 2 * DI], F32, kind="ExternalInput").ap()
    c.fW2 = nc.dram_tensor("fW2", [DEPTH, DI, D], F32, kind="ExternalInput").ap()
    c.f_b2 = nc.dram_tensor("f_b2", [DEPTH, D], F32, kind="ExternalInput").ap()
    c.out = nc.dram_tensor("out", [S, D], F32, kind="ExternalOutput").ap()

    c.d_QT = nc.dram_tensor("d_QT", [D, S], BF16).ap()
    c.d_KT = nc.dram_tensor("d_KT", [D, S], BF16).ap()
    c.d_V = nc.dram_tensor("d_V", [S, D], BF16).ap()
    c.d_x1 = nc.dram_tensor("d_x1", [S, D], F32).ap()
    c.d_z = [nc.dram_tensor("d_za", [S, D], F32).ap(),
             nc.dram_tensor("d_zb", [S, D], F32).ap()]
    c.d_rows = nc.dram_tensor("d_rows", [4, 32, 128], BF16).ap()
    c.zero_rows = nc.dram_tensor("zero_rows", [4 * 32 * 128], F32,
                                 kind="ExternalInput").ap()
    c.r_cq = nc.dram_tensor("r_cq", [128, S], F32, kind="ExternalInput").ap()
    c.r_sq = nc.dram_tensor("r_sq", [128, S], F32, kind="ExternalInput").ap()
    c.r_ck = nc.dram_tensor("r_ck", [128, S], F32, kind="ExternalInput").ap()
    c.r_sk = nc.dram_tensor("r_sk", [128, S], F32, kind="ExternalInput").ap()

    c.dbg = {}
    for name, shape, dt in debug:
        c.dbg[name] = nc.dram_tensor(name, shape, dt, kind="ExternalOutput").ap()

    for nm in SEM_NAMES:
        setattr(c, nm, Sem(es.enter_context(nc.semaphore(nm))))

    c.identb = sb("identb", [128, 128], BF16)
    c.identf = sb("identf", [128, 128], F32)
    c.iota_c = sb("iota_c", [128, 1], F32)
    c.iota_r = sb("iota_r", [128, 512], F32)
    c.mskA = sb("mskA", [128, 512], F32)
    c.mskB = sb("mskB", [128, 512], F32)
    c.ones_row = sb("ones_row", [1, 1024], BF16)
    c.onesLUT = sb("onesLUT", [128, 65], BF16)
    c.sel16 = sb("sel16", [16, 2048], BF16)
    c.Mqk = sb("Mqk", [128, 1024], BF16)
    c.Mqk0 = sb("Mqk0", [128, 512], BF16)
    c.MT = sb("MT", [128, 1024], BF16)
    c.MT0 = sb("MT0", [128, 512], BF16)
    c.MTdiag = sb("MTdiag", [128, 128], BF16)
    c.cst = sb("cst", [128, 12], F32)
    c.nlut = sb("nlut", [128, 1024], BF16)
    c.bsel = sb("bsel", [128, 1024], BF16)
    c.sgn_col = sb("sgn_col", [128, 1], F32)
    c.ones_f32 = sb("ones_f32", [1, 128], F32)
    c.e64_col = sb("e64_col", [128, 1], F32)
    c.e32_col = sb("e32_col", [128, 1], F32)
    c.Pmat = sb("Pmat", [128, 128], BF16)

    with nc.Block("init") as blk:
        @blk.gpsimd
        def _(g):
            g.iota(c.iota_c[:], pattern=[[1, 1]], base=0, channel_multiplier=1,
                   allow_small_or_imprecise_dtypes=True)
            g.iota(c.iota_r[:], pattern=[[1, 512]], base=0, channel_multiplier=0,
                   allow_small_or_imprecise_dtypes=True)
            g.memset(c.ones_row[:], 1.0)
            c.po_b.inc(g.memset(c.onesLUT[:], 0.0), 1)
            g.wait_ge(c.po_b.h, c.po_b.n)
            g.memset(c.onesLUT[0:64, 0:1], 1.0)
            g.memset(c.onesLUT[64:128, 64:65], 1.0)
            g.memset(c.cst[:, 0:1], LN_EPS)
            g.memset(c.cst[:, 1:2], -0.5)
            g.memset(c.cst[:, 2:3], -OUTER_C)
            g.memset(c.cst[:, 3:4], 0.125)
            g.memset(c.cst[:, 4:5], 8.0)
            g.memset(c.cst[:, 5:6], -np.log(10000.0) / 32.0)
            g.memset(c.cst[:, 6:7], 25.6 / 89.6)
            g.memset(c.cst[:, 7:8], 2.0 / 89.6)
            g.memset(c.cst[:, 8:9], -1.0)
            g.memset(c.cst[:, 9:10], 1.0)
            g.memset(c.cst[:, 10:11], 0.5)
            g.memset(c.ones_f32[:], 1.0)
            c.po_a.inc(g.memset(c.sel16[:], 0.0), 1)

        @blk.vector
        def _(v):
            v.wait_ge(c.po_a.h, c.po_a.n)
            v.tensor_scalar(c.mskA[:, 0:128], c.iota_r[:, 0:128], c.iota_c[:],
                            None, ALU.subtract)
            v.tensor_scalar(c.identf[:], c.mskA[:, 0:128], 0.0, None,
                            ALU.is_equal)
            v.tensor_copy(c.identb[:], c.identf[:])
            v.tensor_scalar(c.mskB[:, 0:128], c.iota_r[:, 0:128], 0.0,
                            c.iota_c[:], ALU.mult, ALU.add)
            for h in range(16):
                v.tensor_scalar(c.sel16[:, 128 * h:128 * (h + 1)],
                                c.mskB[0:16, 0:128], float(h), None, ALU.is_equal)
            for q_ in range(2):
                _band(v, c.Mqk[:, 512 * q_:512 * (q_ + 1)], 512,
                      128 * q_, 128 * q_ + 256, c)
            for q_ in range(2):
                _band(v, c.Mqk0[:, 256 * q_:256 * (q_ + 1)], 256, -BIG, 128 * q_, c)
            for q_ in range(2):
                for cc in range(4):
                    _band(v, c.MT[:, 512 * q_ + 128 * cc:512 * q_ + 128 * (cc + 1)],
                          128, 128 * (cc - q_) - 256, 128 * (cc - q_), c)
            for q_ in range(2):
                for cc in range(2):
                    _band(v, c.MT0[:, 256 * q_ + 128 * cc:256 * q_ + 128 * (cc + 1)],
                          128, 128 * (cc - q_), BIG, c)
            _band(v, c.MTdiag[:], 128, 0, BIG, c)
            half_lo = c.mskA[:, 1:2]
            v.tensor_scalar(half_lo, c.iota_c[:], 64.0, None, ALU.is_lt)
            half_hi = c.mskA[:, 2:3]
            v.tensor_scalar(half_hi, c.iota_c[:], 64.0, None, ALU.is_ge)
            for fc in range(8):
                a = c.mskB[:, 0:128]
                b = c.mskB[:, 128:256]
                v.tensor_scalar(a, c.iota_r[:, 0:128], float(fc), None, ALU.is_equal)
                v.tensor_scalar(a, a, half_lo, None, ALU.mult)
                v.tensor_scalar(b, c.iota_r[:, 0:128], float(64 + fc), None,
                                ALU.is_equal)
                v.tensor_scalar(b, b, half_hi, None, ALU.mult)
                v.tensor_tensor(c.nlut[:, 128 * fc:128 * (fc + 1)], a, b, ALU.add)
                pa = c.mskA[:, 3:4]
                v.tensor_scalar(pa, c.iota_c[:], float(fc), None, ALU.is_equal)
                v.tensor_scalar(a, c.iota_r[:, 0:128], 64.0, None, ALU.is_lt)
                v.tensor_scalar(a, a, pa, None, ALU.mult)
                rb = c.mskB[:, 256:384]
                v.tensor_scalar(pa, c.iota_c[:], float(64 + fc), None, ALU.is_equal)
                v.tensor_scalar(rb, c.iota_r[:, 0:128], 64.0, None, ALU.is_ge)
                v.tensor_scalar(rb, rb, pa, None, ALU.mult)
                v.tensor_tensor(c.bsel[:, 128 * fc:128 * (fc + 1)], a, rb, ALU.add)
            v.tensor_scalar(c.mskA[:, 0:128], c.iota_r[:, 0:128], 64.0, -64.0,
                            ALU.is_ge, ALU.mult)
            v.tensor_tensor(c.mskA[:, 0:128], c.iota_r[:, 0:128],
                            c.mskA[:, 0:128], ALU.add)
            v.tensor_scalar(c.mskA[:, 0:128], c.mskA[:, 0:128], 32.0, -64.0,
                            ALU.is_ge, ALU.mult)
            v.tensor_tensor(c.mskA[:, 0:128], c.iota_r[:, 0:128],
                            c.mskA[:, 0:128], ALU.add)
            v.tensor_scalar(c.mskA[:, 0:128], c.mskA[:, 0:128], 32.0, None,
                            ALU.add)
            v.tensor_scalar(c.mskA[:, 0:128], c.mskA[:, 0:128], c.iota_c[:],
                            None, ALU.subtract)
            v.tensor_scalar(c.Pmat[:], c.mskA[:, 0:128], 0.0, None,
                            ALU.is_equal)
            c.dv_a.inc(v.tensor_copy(c.mskB[:, 0:1], c.iota_c[:]), 1)
    c.init_done = c.dv_a.n

    if "p1" in phases:
        p1_outer_qkv(c)
    if "p2" in phases:
        p2_outer_attn(c)
    if "p3" in phases:
        p3_outer_proj(c)
    if "inner" in phases:
        for li in range(DEPTH):
            pa_inner_attn(c, li)
            pb_inner_ffn(c, li, final=(li == DEPTH - 1))
    else:
        for ph in phases:
            if ph.startswith("pa"):
                pa_inner_attn(c, int(ph[2:]))
            elif ph.startswith("pb"):
                pb_inner_ffn(c, int(ph[2:].rstrip("f")),
                             final=(ph.endswith("f")))
    _final_wait(c)

    if hasattr(c, "p23_stack"):
        c.p23_stack.close()
    es.close()
    return nc, c


def p1_outer_qkv(c):
    upto = getattr(c, 'p1_upto', 4)
    nc = c.nc
    st = contextlib.ExitStack()

    def sb(name, shape, dt):
        return st.enter_context(nc.sbuf_tensor(name, shape, dt))

    def psb(name, shape, dt=F32):
        return st.enter_context(nc.psum_tensor(name, shape, dt))

    Wraw = sb("p1_Wraw", [128, 24 * D], BF16)
    WT = sb("p1_WT", [128, 24 * D], BF16)
    brow = sb("p1_brow", [1, 3 * D], BF16)
    xt = [sb(f"p1_x{i}", [128, D], F32) for i in range(4)]
    xn = [sb(f"p1_xn{i}", [128, D], BF16) for i in range(4)]
    fm = sb("p1_fm", [128, 8 * SEG], BF16)
    ev = [sb(f"p1_ev{i}", [128, SEG], BF16) for i in range(4)]
    vev = [sb(f"p1_vev{i}", [128, D], BF16) for i in range(2)]
    statg = sb("p1_stat", [128, 28], F32)
    sq = sb("p1_sq", [128, D], F32)

    ps_t = [psb(f"p1_pst{i}", [128, 1024], BF16) for i in range(2)]
    ps_qk = [psb(f"p1_psqk{i}", [128, 512]) for i in range(2)]
    ps_v = [psb(f"p1_psv{i}", [128, D]) for i in range(2)]

    B = c.bases()
    with nc.Block("p1w") as blk:
        @blk.gpsimd
        def _(g):
            c.gd_a.inc(g.dma_start(Wraw[:].rearrange("p (c d) -> p c d", c=24),
                                   c.W_qkv.rearrange("(c p) d -> p c d", p=128)), 16)
            c.gd_a.inc(g.dma_start(brow[0:1, :], c.b_qkv[None, :]), 16)

        @blk.tensor
        def _(t):
            t.wait_ge(c.gd_a.h, B["gd_a"] + 32)
            t.wait_ge(c.dv_a.h, c.init_done)
            gi = 0
            for dc in range(8):
                for fcg in range(6):
                    if gi >= 2:
                        t.wait_ge(c.dv_c.h, B["dv_c"] + gi - 1)
                    for k in range(4):
                        fc = 4 * fcg + k
                        ins = nc.tensor.transpose(
                            ps_t[gi % 2][:, 128 * k:128 * (k + 1)],
                            Wraw[:, 1024 * fc + 128 * dc:1024 * fc + 128 * (dc + 1)],
                            c.identb[:])
                        if k == 3:
                            c.pe_a.inc(ins, 1)
                    gi += 1

        @blk.vector
        def _(v):
            for gi in range(48):
                dc, fcg = gi // 6, gi % 6
                v.wait_ge(c.pe_a.h, B["pe_a"] + gi + 1)
                c.dv_c.inc(v.tensor_copy(
                    WT[:, 3072 * dc + 512 * fcg:3072 * dc + 512 * (fcg + 1)],
                    ps_t[gi % 2][:, 0:512]), 1)

    B = c.bases()

    def qk_store_cnt(g_, fc):
        return 16 * (20 * g_ + fc + 1)

    def v_store_cnt(tau):
        return 16 * (20 * (tau // 4) + 17 + (tau % 4))

    if upto < 2:
        if "dbg_wt" in c.dbg:
            with nc.Block("p1dbgw") as blk:
                @blk.gpsimd
                def _(g):
                    g.wait_ge(c.dv_c.h, c.dv_c.n)
                    c.gd_b.inc(g.dma_start(c.dbg["dbg_wt"][:], WT[:]), 16)
                    g.wait_ge(c.gd_b.h, c.gd_b.n)
        st.close()
        return
    with nc.Block("p1m") as blk:
        if "dbg_xn" in c.dbg:
            @blk.gpsimd
            def _(g):
                g.wait_ge(c.ac_b.h, B["ac_b"] + 4)
                for i in range(4):
                    c.gd_b.inc(g.dma_start(
                        c.dbg["dbg_xn"][128 * i:128 * (i + 1), :], xn[i][:]), 16)
                c.gd_b.inc(g.dma_start(c.dbg["dbg_st"][:], statg[:]), 16)
                g.wait_ge(c.gd_b.h, c.gd_b.n)

        @blk.sync
        def _(sy):
            for ti in range(4):
                c.yd_a.inc(sy.dma_start(xt[ti][:], c.x_in[128 * ti:128 * (ti + 1), :]),
                           16)
            for g_ in range(NSEG):
                for fc in (range(16) if upto >= 4 else []):
                    dst = c.d_QT if fc < 8 else c.d_KT
                    r0 = 128 * (fc % 8)
                    if fc % 2 == 0:
                        sy.wait_ge(c.dv_d.h, B["dv_d"] + 8 * g_ + fc // 2 + 1)
                    else:
                        sy.wait_ge(c.ac_c.h, B["ac_c"] + 8 * g_ + fc // 2 + 1)
                    c.yd_b.inc(sy.dma_start(
                        dst[r0:r0 + 128, SEG * g_:SEG * (g_ + 1)], ev[fc % 4][:]), 16)
                for ti in (range(4) if upto >= 4 else []):
                    tau = 4 * g_ + ti
                    sy.wait_ge(c.dv_e.h, B["dv_e"] + tau + 1)
                    c.yd_b.inc(sy.dma_start(
                        c.d_V[128 * tau:128 * (tau + 1), :], vev[tau % 2][:]), 16)
                if g_ + 1 < NSEG:
                    for ti in range(4):
                        tau = 4 * (g_ + 1) + ti
                        if upto >= 2:
                            sy.wait_ge(c.ac_b.h, B["ac_b"] + tau - 4 + 1)
                        c.yd_a.inc(sy.dma_start(
                            xt[tau % 4][:], c.x_in[128 * tau:128 * (tau + 1), :]), 16)

        @blk.vector
        def _(v):
            for g_ in range(NSEG):
                for ti in range(4):
                    tau = 4 * g_ + ti
                    v.wait_ge(c.yd_a.h, B["yd_a"] + 16 * (tau + 1))
                    x_ = xt[tau % 4]
                    stt = statg[:, 7 * ti:7 * ti + 7]
                    v.tensor_reduce(stt[:, 0:1], x_[:], AX.X, ALU.add)
                    v.scalar_tensor_tensor(sq[:], x_[:], 1.0, x_[:], ALU.mult,
                                           ALU.mult, accum_out=stt[:, 1:2])
                    v.tensor_scalar(stt[:, 2:3], stt[:, 0:1], 1.0 / D, None, ALU.mult)
                    v.tensor_scalar(stt[:, 3:4], stt[:, 1:2], 1.0 / D, None, ALU.mult)
                    v.tensor_tensor(stt[:, 4:5], stt[:, 2:3], stt[:, 2:3], ALU.mult)
                    v.tensor_copy(sq[:, 0:1], c.iota_c[:])
                    c.dv_a.inc(v.tensor_tensor(stt[:, 3:4], stt[:, 3:4], stt[:, 4:5],
                                               ALU.subtract), 1)
                    v.wait_ge(c.ac_a.h, B["ac_a"] + tau + 1)
                    c.dv_b.inc(v.tensor_scalar(stt[:, 6:7], stt[:, 2:3],
                                               stt[:, 5:6], -1.0, ALU.mult,
                                               ALU.mult), 1)
                for dc in (range(8) if upto >= 3 else []):
                    v.wait_ge(c.pe_a.h, B["pe_a"] + 8 * g_ + dc + 1)
                    c.dv_c.inc(v.tensor_copy(fm[:, 512 * dc:512 * (dc + 1)],
                                             ps_t[dc % 2][:, 0:512]), 1)
                for fc in (range(0, 16, 2) if upto >= 4 else []):
                    v.wait_ge(c.pe_b.h, B["pe_b"] + 16 * g_ + fc + 1)
                    if fc >= 4 or g_ > 0:
                        pf = qk_store_cnt(g_, fc - 4) if fc >= 4 \
                            else qk_store_cnt(g_ - 1, 16 + 3)
                        v.wait_ge(c.yd_b.h, B["yd_b"] + pf)
                    c.dv_d.inc(v.tensor_copy(ev[fc % 4][:], ps_qk[0][:]), 1)
                for ti in (range(4) if upto >= 4 else []):
                    tau = 4 * g_ + ti
                    v.wait_ge(c.pe_c.h, B["pe_c"] + tau + 1)
                    if tau >= 2:
                        v.wait_ge(c.yd_b.h, B["yd_b"] + v_store_cnt(tau - 2))
                    c.dv_e.inc(v.tensor_copy(vev[tau % 2][:], ps_v[tau % 2][:]), 1)

        @blk.scalar
        def _(sc):
            for g_ in range(NSEG):
                for ti in range(4):
                    tau = 4 * g_ + ti
                    stt = statg[:, 7 * ti:7 * ti + 7]
                    sc.wait_ge(c.dv_a.h, B["dv_a"] + tau + 1)
                    sc.activation(stt[:, 5:6], stt[:, 3:4], AF.Ln, bias=c.cst[:, 0:1],
                                  scale=1.0)
                    sc.activation(stt[:, 5:6], stt[:, 5:6], AF.Exp,
                                  bias=0.0, scale=c.cst[:, 1:2])
                    c.ac_a.inc(sc.drain(), 1)
                    sc.wait_ge(c.dv_b.h, B["dv_b"] + tau + 1)
                    if tau >= 4 and upto >= 3:
                        sc.wait_ge(c.pe_a.h, B["pe_a"] + 8 * (tau // 4))
                    sc.activation(xn[tau % 4][:], xt[tau % 4][:],
                                  AF.Identity, bias=stt[:, 6:7],
                                  scale=stt[:, 5:6])
                    c.ac_b.inc(sc.drain(), 1)
                for fc in (range(1, 16, 2) if upto >= 4 else []):
                    sc.wait_ge(c.pe_b.h, B["pe_b"] + 16 * g_ + fc + 1)
                    if fc >= 4 or g_ > 0:
                        pf = qk_store_cnt(g_, fc - 4) if fc >= 4 \
                            else qk_store_cnt(g_ - 1, 16 + 3)
                        sc.wait_ge(c.yd_b.h, B["yd_b"] + pf)
                    sc.copy(ev[fc % 4][:], ps_qk[1][:])
                    c.ac_c.inc(sc.drain(), 1)

        @blk.tensor
        def _(t):
            if upto < 3:
                return
            t.wait_ge(c.dv_c.h, B["dv_c"])
            for g_ in range(NSEG):
                for dc in range(8):
                    gi = 8 * g_ + dc
                    if gi >= 2:
                        t.wait_ge(c.dv_c.h, B["dv_c"] + gi - 1)
                    for ti in range(4):
                        tau = 4 * g_ + ti
                        t.wait_ge(c.ac_b.h, B["ac_b"] + tau + 1)
                        ins = nc.tensor.transpose(
                            ps_t[dc % 2][:, 128 * ti:128 * (ti + 1)],
                            xn[tau % 4][:, 128 * dc:128 * (dc + 1)], c.identb[:])
                        if ti == 3:
                            c.pe_a.inc(ins, 1)
                if upto < 4:
                    continue
                t.wait_ge(c.dv_c.h, B["dv_c"] + 8 * (g_ + 1))
                for fc in range(16):
                    pq = ps_qk[fc % 2]
                    if 16 * g_ + fc >= 2:
                        idx = 16 * g_ + fc - 2
                        if fc % 2 == 0:
                            t.wait_ge(c.dv_d.h, B["dv_d"] + idx // 2 + 1)
                        else:
                            t.wait_ge(c.ac_c.h, B["ac_c"] + idx // 2 + 1)
                    for dc in range(8):
                        nc.tensor.matmul(
                            pq[:], WT[:, 3072 * dc + 128 * fc:3072 * dc + 128 * (fc + 1)],
                            fm[:, 512 * dc:512 * (dc + 1)],
                            start=(dc == 0), stop=False)
                    c.pe_b.inc(nc.tensor.matmul(
                        pq[:], brow[0:1, 128 * fc:128 * (fc + 1)],
                        c.ones_row[0:1, 0:512], start=False, stop=True), 1)
                for ti in range(4):
                    tau = 4 * g_ + ti
                    pv = ps_v[tau % 2]
                    if tau >= 2:
                        t.wait_ge(c.dv_e.h, B["dv_e"] + tau - 1)
                    for half in range(2):
                        col0 = 2048 + 512 * half
                        for dc in range(8):
                            nc.tensor.matmul(
                                pv[:, 512 * half:512 * (half + 1)],
                                fm[:, 512 * dc + 128 * ti:512 * dc + 128 * (ti + 1)],
                                WT[:, 3072 * dc + col0:3072 * dc + col0 + 512],
                                start=(dc == 0), stop=False)
                        ins = nc.tensor.matmul(
                            pv[:, 512 * half:512 * (half + 1)],
                            c.ones_row[0:1, 0:128], brow[0:1, col0:col0 + 512],
                            start=False, stop=True)
                        if half == 1:
                            c.pe_c.inc(ins, 1)

    if "dbg_qt" in c.dbg:
        with nc.Block("p1dbg") as blk:
            @blk.gpsimd
            def _(g):
                g.wait_ge(c.yd_b.h, c.yd_b.n)
                c.gd_b.inc(g.dma_start(c.dbg["dbg_qt"][:], c.d_QT[:]), 16)
                c.gd_b.inc(g.dma_start(c.dbg["dbg_kt"][:], c.d_KT[:]), 16)
                c.gd_b.inc(g.dma_start(c.dbg["dbg_v"][:], c.d_V[:]), 16)
                g.wait_ge(c.gd_b.h, c.gd_b.n)

    st.close()
    c.p1_done = dict(yd_b=c.yd_b.n)


def p2_outer_attn(c):
    nc = c.nc
    st = contextlib.ExitStack()

    def sb(name, shape, dt):
        return st.enter_context(nc.sbuf_tensor(name, shape, dt))

    def psb(name, shape, dt=F32):
        return st.enter_context(nc.psum_tensor(name, shape, dt))

    c.p23_stack = contextlib.ExitStack()
    VB = 194
    c.outT = c.p23_stack.enter_context(
        nc.sbuf_tensor("p2_outT", [128, 8 * S], BF16))
    KT = [sb(f"p2_kt{i}", [128, S], BF16) for i in range(2)]
    QT = [sb(f"p2_qt{i}", [128, S], BF16) for i in range(2)]
    VS = [sb(f"p2_vs{i}", [128, 32 * VB], BF16) for i in range(2)]
    pT = [sb(f"p2_pt{i}", [128, 1024], BF16) for i in range(2)]
    rc = sb("p2_rc", [1, 256], F32)
    bcs = sb("p2_bcs", [128, 256], F32)
    onesA = sb("p2_onesA", [1, 64], F32)
    onesB = sb("p2_onesB", [1, 128], F32)

    ps_sT = [psb(f"p2_pssT{i}", [128, 1024]) for i in range(2)]
    ps_oA = psb("p2_psoA", [65, 512])
    ps_oB = psb("p2_psoB", [128, 512])
    ps_bc = psb("p2_psbc", [128, 512])
    dbgS = sb("p2_dbgS", [128, 1280], F32) if "dbg_p2s" in c.dbg else None

    B = c.bases()
    ngrp = [(qt // 4) + 1 for qt in range(32)]
    grp_pref = [0]
    for qt in range(32):
        grp_pref.append(grp_pref[-1] + ngrp[qt])
    GPP = grp_pref[32]

    with nc.Block("p2") as blk:
        @blk.gpsimd
        def _(g):
            g.memset(VS[0][:], 0.0)
            g.memset(VS[1][:], 0.0)
            g.memset(VS[0][:].rearrange("p (k c) -> p k c", c=VB)[:, :, 32:33], 1.0)
            g.memset(VS[0][:].rearrange("p (k c) -> p k c", c=VB)[:, :, 193:194], 1.0)
            g.memset(VS[1][:].rearrange("p (k c) -> p k c", c=VB)[:, :, 32:33], 1.0)
            g.memset(VS[1][:].rearrange("p (k c) -> p k c", c=VB)[:, :, 193:194], 1.0)
            g.memset(onesA[:], 1.0)
            g.memset(onesB[0:1, 0:64], 0.0)
            c.po_a.inc(g.memset(onesB[0:1, 64:128], 1.0), 1)

        @blk.sync
        def _(sy):
            sy.wait_ge(c.yd_b.h, B["yd_b"])
            for p in range(8):
                if p >= 2:
                    sy.wait_ge(c.dv_b.h, B["dv_b"] + 32 * (p - 1))
                sy.wait_ge(c.po_a.h, B["po_a"] + 1)
                c.yd_a.inc(sy.dma_start(KT[p % 2][:],
                                        c.d_KT[128 * p:128 * (p + 1), :]), 16)
                c.yd_a.inc(sy.dma_start(QT[p % 2][:],
                                        c.d_QT[128 * p:128 * (p + 1), :]), 16)
                va = VS[p % 2][:].rearrange("p (k c) -> p k c", c=VB)
                c.yd_a.inc(sy.dma_start(
                    va[:, :, 129:193],
                    c.d_V.rearrange("(k p) e -> p k e", p=128)[:, :, 128 * p:128 * p + 64]), 16)
                c.yd_a.inc(sy.dma_start(
                    va[:, :, 64:128],
                    c.d_V.rearrange("(k p) e -> p k e", p=128)[:, :, 128 * p + 64:128 * p + 128]), 16)

        @blk.tensor
        def _(t):
            t.wait_ge(c.po_a.h, B["po_a"] + 1)
            gcnt = 0
            pvcnt = 0
            for p in range(8):
                t.wait_ge(c.yd_a.h, B["yd_a"] + 64 * (p + 1))
                for qt in range(32):
                    for gi in range(ngrp[qt]):
                        kcs = range(4 * gi, min(4 * gi + 4, qt + 1))
                        pss = ps_sT[gcnt % 2]
                        if gcnt >= 2:
                            t.wait_ge(c.ac_a.h, B["ac_a"] + gcnt - 1)
                        first = True
                        for kc in kcs:
                            ca = 128 * (kc % 4)
                            if kc == qt:
                                t.matmul(pss[:, ca:ca + 128], c.identb[:],
                                         c.MTdiag[:], start=True, stop=False)
                                t.matmul(pss[:, 512 + ca:512 + ca + 128],
                                         c.identb[:], c.MTdiag[:],
                                         start=True, stop=False)
                                st_ = False
                            else:
                                st_ = True
                            t.matmul(pss[:, ca:ca + 128],
                                     KT[p % 2][0:64, 128 * kc:128 * (kc + 1)],
                                     QT[p % 2][0:64, 128 * qt:128 * (qt + 1)],
                                     start=st_, stop=True, tile_position=(0, 0))
                            ins = t.matmul(pss[:, 512 + ca:512 + ca + 128],
                                           KT[p % 2][64:128, 128 * kc:128 * (kc + 1)],
                                           QT[p % 2][64:128, 128 * qt:128 * (qt + 1)],
                                           start=st_, stop=True,
                                           tile_position=(64, 0))
                        c.pe_b.inc(ins, 1)
                        gcnt += 1
                        if gi == 0:
                            t.wait_ge(c.dv_b.h, B["dv_b"] + 32 * p + qt)
                        t.wait_ge(c.ac_a.h, B["ac_a"] + gcnt)
                        va = VS[p % 2][:].rearrange("p (k c) -> p k c", c=VB)
                        for kc in kcs:
                            ca = 128 * (kc % 4)
                            t.matmul(ps_oA[:, 0:128], va[:, kc, 129:194],
                                     pT[(gcnt - 1) % 2][:, ca:ca + 128],
                                     start=(kc == kcs[0] and gi == 0),
                                     stop=(kc == qt))
                            ins = t.matmul(ps_oB[:, 0:128], va[:, kc, 0:128],
                                           pT[(gcnt - 1) % 2][:, 512 + ca:512 + ca + 128],
                                           start=(kc == kcs[0] and gi == 0),
                                           stop=(kc == qt))
                        c.pe_c.inc(ins, 1)
                        pvcnt += 1
                    iq = 32 * p + qt
                    t.wait_ge(c.dv_a.h, B["dv_a"] + iq + 1)
                    t.matmul(ps_bc[0:64, 0:128], onesA[:], rc[0:1, 0:128],
                             start=True, stop=True)
                    c.pe_d.inc(t.matmul(ps_bc[:, 128:256], onesB[:],
                                        rc[0:1, 128:256], start=True, stop=True), 1)

        @blk.scalar
        def _(sc):
            gcnt = 0
            for p in range(8):
                for qt in range(32):
                    for gi in range(ngrp[qt]):
                        klen = min(4 * gi + 4, qt + 1) - 4 * gi
                        sc.wait_ge(c.pe_b.h, B["pe_b"] + gcnt + 1)
                        if gcnt >= 2:
                            sc.wait_ge(c.pe_c.h, B["pe_c"] + gcnt - 1)
                        po = pT[gcnt % 2][:].rearrange("p (g x) -> p g x", g=2)
                        pi = ps_sT[gcnt % 2][:].rearrange("p (g x) -> p g x", g=2)
                        sc.activation(po[:, :, 0:128 * klen], pi[:, :, 0:128 * klen],
                                      AF.Exp, bias=c.cst[:, 2:3], scale=c.cst[:, 3:4])
                        c.ac_a.inc(sc.drain(), 1)
                        gcnt += 1

        @blk.vector
        def _(v):
            for p in range(8):
                for qt in range(32):
                    iq = 32 * p + qt
                    if iq == 0 and "dbg_p2s" in c.dbg:
                        v.wait_ge(c.pe_b.h, B["pe_b"] + 1)
                        v.tensor_copy(dbgS[:, 0:128], ps_sT[0][:, 0:128])
                        v.tensor_copy(dbgS[:, 128:256], ps_sT[0][:, 512:640])
                        v.wait_ge(c.ac_a.h, B["ac_a"] + 1)
                        v.tensor_copy(dbgS[:, 256:384], pT[0][:, 0:128])
                        v.tensor_copy(dbgS[:, 384:512], pT[0][:, 512:640])
                    v.wait_ge(c.pe_c.h, B["pe_c"] + grp_pref[qt + 1] + GPP * p)
                    if iq >= 1:
                        v.wait_ge(c.pe_d.h, B["pe_d"] + iq)
                    if iq == 0 and "dbg_p2s" in c.dbg:
                        v.tensor_copy(dbgS[0:65, 512:640], ps_oA[0:65, 0:128])
                        v.tensor_copy(dbgS[:, 640:768], ps_oB[:, 0:128])
                    v.reciprocal(rc[0:1, 0:128], ps_oA[64:65, 0:128])
                    c.dv_a.inc(v.reciprocal(rc[0:1, 128:256], ps_oB[32:33, 0:128]), 1)
                    if iq == 0 and "dbg_p2s" in c.dbg:
                        v.tensor_copy(dbgS[0:1, 768:1024], rc[0:1, 0:256])
                    v.wait_ge(c.pe_d.h, B["pe_d"] + iq + 1)
                    if iq == 0 and "dbg_p2s" in c.dbg:
                        v.tensor_copy(dbgS[:, 1024:1280], ps_bc[:, 0:256])
                    v.tensor_copy(bcs[0:64, 0:128], ps_bc[0:64, 0:128])
                    v.tensor_copy(bcs[64:128, 128:256], ps_bc[64:128, 128:256])
                    o = c.outT[:, 4096 * p + 128 * qt:4096 * p + 128 * (qt + 1)]
                    v.tensor_tensor(o[0:64, :], ps_oA[0:64, 0:128], bcs[0:64, 0:128],
                                    ALU.mult)
                    c.dv_b.inc(v.tensor_tensor(o[64:128, :], ps_oB[64:128, 0:128],
                                               bcs[64:128, 128:256], ALU.mult), 1)

    if "dbg_ot" in c.dbg:
        with nc.Block("p2dbg") as blk:
            @blk.gpsimd
            def _(g):
                g.wait_ge(c.dv_b.h, c.dv_b.n)
                c.gd_b.inc(g.dma_start(c.dbg["dbg_ot"][:], c.outT[:]), 16)
                if "dbg_p2s" in c.dbg:
                    c.gd_b.inc(g.dma_start(c.dbg["dbg_p2s"][:], dbgS[:]), 16)
                g.wait_ge(c.gd_b.h, c.gd_b.n)

    st.close()


def p3_outer_proj(c):
    nc = c.nc
    st = contextlib.ExitStack()

    def sb(name, shape, dt):
        return st.enter_context(nc.sbuf_tensor(name, shape, dt))

    def psb(name, shape, dt=F32):
        return st.enter_context(nc.psum_tensor(name, shape, dt))

    WoT = sb("p3_wot", [128, 8 * D], BF16)
    Wraw = sb("p3_wraw", [128, 8 * D], BF16)
    borow = sb("p3_bo", [1, D], BF16)
    xt = [sb(f"p3_x{i}", [128, D], F32) for i in range(2)]
    ysb = [sb(f"p3_y{i}", [128, D], F32) for i in range(2)]
    x1t = [sb(f"p3_x1{i}", [128, D], F32) for i in range(2)]
    z0t = [sb(f"p3_z0{i}", [128, D], F32) for i in range(2)]
    stat = sb("p3_stat", [128, 16], F32)
    sq = sb("p3_sq", [128, D], F32)
    ps_t = [psb(f"p3_pst{i}", [128, 1024], BF16) for i in range(2)]
    ps_o = [psb(f"p3_pso{i}", [128, D]) for i in range(2)]

    B = c.bases()
    with nc.Block("p3") as blk:
        @blk.gpsimd
        def _(g):
            c.gd_a.inc(g.dma_start(Wraw[:].rearrange("p (c d) -> p c d", c=8),
                                   c.W_o.rearrange("(c p) d -> p c d", p=128)), 16)
            c.gd_a.inc(g.dma_start(borow[0:1, :], c.b_o[None, :]), 16)

        @blk.tensor
        def _(t):
            t.wait_ge(c.gd_a.h, c.gd_a.n)
            gi = 0
            for dc in range(8):
                for ec in range(8):
                    if gi >= 2:
                        t.wait_ge(c.dv_a.h, B["dv_a"] + gi - 1)
                    ins = nc.tensor.transpose(
                        ps_t[gi % 2][:, 0:128],
                        Wraw[:, 1024 * dc + 128 * ec:1024 * dc + 128 * (ec + 1)],
                        c.identb[:])
                    c.pe_a.inc(ins, 1)
                    gi += 1
            for tau in range(NT):
                po = ps_o[tau % 2]
                if tau >= 2:
                    t.wait_ge(c.ac_a.h, B["ac_a"] + tau - 1)
                for half in range(2):
                    for ec in range(8):
                        t.matmul(po[:, 512 * half:512 * (half + 1)],
                                 c.outT[:, 4096 * ec + 128 * tau:4096 * ec + 128 * (tau + 1)],
                                 WoT[:, 1024 * ec + 512 * half:1024 * ec + 512 * (half + 1)],
                                 start=(ec == 0), stop=False)
                    ins = t.matmul(po[:, 512 * half:512 * (half + 1)],
                                   c.ones_row[0:1, 0:128],
                                   borow[0:1, 512 * half:512 * (half + 1)],
                                   start=False, stop=True)
                c.pe_b.inc(ins, 1)

        @blk.vector
        def _(v):
            gi = 0
            for dc in range(8):
                for ec in range(8):
                    v.wait_ge(c.pe_a.h, B["pe_a"] + gi + 1)
                    c.dv_a.inc(v.tensor_copy(
                        WoT[:, 1024 * ec + 128 * dc:1024 * ec + 128 * (dc + 1)],
                        ps_t[gi % 2][:, 0:128]), 1)
                    gi += 1
            for tau in range(NT):
                stt = stat[:, 0:16]
                v.wait_ge(c.ac_a.h, B["ac_a"] + tau + 1)
                y = ysb[tau % 2]
                v.tensor_reduce(stt[:, 0:1], y[:], AX.X, ALU.add)
                v.scalar_tensor_tensor(sq[:], y[:], 1.0, y[:], ALU.mult, ALU.mult,
                                       accum_out=stt[:, 1:2])
                v.tensor_scalar(stt[:, 2:3], stt[:, 0:1], 1.0 / D, None, ALU.mult)
                v.tensor_scalar(stt[:, 3:4], stt[:, 1:2], 1.0 / D, None, ALU.mult)
                v.tensor_tensor(stt[:, 4:5], stt[:, 2:3], stt[:, 2:3], ALU.mult)
                v.tensor_copy(sq[:, 0:1], c.iota_c[:])
                c.dv_b.inc(v.tensor_tensor(stt[:, 3:4], stt[:, 3:4], stt[:, 4:5],
                                           ALU.subtract), 1)
                v.wait_ge(c.ac_b.h, B["ac_b"] + tau + 1)
                c.dv_c.inc(v.tensor_scalar(stt[:, 6:7], stt[:, 2:3], stt[:, 5:6],
                                           -1.0, ALU.mult, ALU.mult), 1)
                v.wait_ge(c.ac_c.h, B["ac_c"] + tau + 1)
                v.wait_ge(c.yd_a.h, B["yd_a"] + 16 * (tau + 1))
                if tau >= 2:
                    v.wait_ge(c.yd_b.h, B["yd_b"] + 16 * (2 * (tau - 2) + 1))
                    v.wait_ge(c.ac_d.h, B["ac_d"] + 2 * (tau - 2) + 2)
                c.dv_d.inc(v.tensor_tensor(x1t[tau % 2][:], z0t[tau % 2][:],
                                           xt[tau % 2][:], ALU.add), 1)
                x1 = x1t[tau % 2]
                v.tensor_reduce(stt[:, 8:9], x1[:], AX.X, ALU.add)
                v.scalar_tensor_tensor(sq[:], x1[:], 1.0, x1[:], ALU.mult, ALU.mult,
                                       accum_out=stt[:, 9:10])
                v.tensor_scalar(stt[:, 10:11], stt[:, 8:9], 1.0 / D, None, ALU.mult)
                v.tensor_scalar(stt[:, 11:12], stt[:, 9:10], 1.0 / D, None, ALU.mult)
                v.tensor_tensor(stt[:, 12:13], stt[:, 10:11], stt[:, 10:11], ALU.mult)
                v.tensor_copy(sq[:, 0:1], c.iota_c[:])
                c.dv_e.inc(v.tensor_tensor(stt[:, 11:12], stt[:, 11:12],
                                           stt[:, 12:13], ALU.subtract), 1)
                v.wait_ge(c.ac_d.h, B["ac_d"] + 2 * tau + 1)
                c.dv_f.inc(v.tensor_scalar(stt[:, 14:15], stt[:, 10:11],
                                           stt[:, 13:14], -1.0, ALU.mult, ALU.mult), 1)

        @blk.scalar
        def _(sc):
            for tau in range(NT):
                sc.wait_ge(c.pe_b.h, B["pe_b"] + tau + 1)
                if tau >= 2:
                    sc.wait_ge(c.dv_b.h, B["dv_b"] + tau - 1)
                sc.activation(ysb[tau % 2][:], ps_o[tau % 2][:], AF.Identity,
                              bias=0.0, scale=1.0)
                c.ac_a.inc(sc.drain(), 1)
                stt = stat[:, 0:16]
                sc.wait_ge(c.dv_b.h, B["dv_b"] + tau + 1)
                sc.activation(stt[:, 5:6], stt[:, 3:4], AF.Ln, bias=c.cst[:, 0:1],
                              scale=1.0)
                sc.activation(stt[:, 5:6], stt[:, 5:6], AF.Exp,
                              bias=0.0, scale=c.cst[:, 1:2])
                c.ac_b.inc(sc.drain(), 1)
                sc.wait_ge(c.dv_c.h, B["dv_c"] + tau + 1)
                if tau >= 2:
                    sc.wait_ge(c.yd_b.h, B["yd_b"] + 16 * (2 * tau - 2))
                sc.activation(z0t[tau % 2][:], ysb[tau % 2][:], AF.Identity,
                              bias=stt[:, 6:7], scale=stt[:, 5:6])
                c.ac_c.inc(sc.drain(), 1)
                sc.wait_ge(c.dv_e.h, B["dv_e"] + tau + 1)
                sc.activation(stt[:, 13:14], stt[:, 11:12], AF.Ln,
                              bias=c.cst[:, 0:1], scale=1.0)
                sc.activation(stt[:, 13:14], stt[:, 13:14], AF.Exp,
                              bias=0.0, scale=c.cst[:, 1:2])
                c.ac_d.inc(sc.drain(), 1)
                sc.wait_ge(c.dv_f.h, B["dv_f"] + tau + 1)
                sc.activation(z0t[tau % 2][:], x1t[tau % 2][:], AF.Identity,
                              bias=stt[:, 14:15], scale=stt[:, 13:14])
                c.ac_d.inc(sc.drain(), 1)

        @blk.sync
        def _(sy):
            for tau in range(NT):
                if tau >= 2:
                    sy.wait_ge(c.dv_d.h, B["dv_d"] + tau - 1)
                c.yd_a.inc(sy.dma_start(xt[tau % 2][:],
                                        c.x_in[128 * tau:128 * (tau + 1), :]), 16)
                sy.wait_ge(c.dv_d.h, B["dv_d"] + tau + 1)
                c.yd_b.inc(sy.dma_start(c.d_x1[128 * tau:128 * (tau + 1), :],
                                        x1t[tau % 2][:]), 16)
                sy.wait_ge(c.ac_d.h, B["ac_d"] + 2 * (tau + 1))
                c.yd_b.inc(sy.dma_start(c.d_z[0][128 * tau:128 * (tau + 1), :],
                                        z0t[tau % 2][:]), 16)

    if "dbg_x1" in c.dbg:
        with nc.Block("p3dbg") as blk:
            @blk.sync
            def _(sy):
                sy.wait_ge(c.yd_b.h, c.yd_b.n)
                c.yd_c.inc(sy.dma_start(c.dbg["dbg_x1"][:], c.d_x1[:]), 16)
                c.yd_c.inc(sy.dma_start(c.dbg["dbg_z0"][:], c.d_z[0][:]), 16)
                sy.wait_ge(c.yd_c.h, c.yd_c.n)

    st.close()
    c.p23_stack.close()


class Sched:

    def __init__(self, base):
        self.n = base
        self.marks = {}

    def tick(self, *key):
        self.n += 1
        self.marks[key] = self.n
        return self.n

    def at(self, *key):
        return self.marks[key]


def pa_inner_attn(c, li):
    nc = c.nc
    st = contextlib.ExitStack()

    def sb(name, shape, dt):
        return st.enter_context(nc.sbuf_tensor(f"{name}_{li}", shape, dt))

    def psb(name, shape, dt=F32):
        return st.enter_context(nc.psum_tensor(f"{name}_{li}", shape, dt))

    L = {}
    VB = 194
    L["Wq"] = sb("pa_wq", [128, 8 * 3 * D], BF16)
    L["Wo"] = sb("pa_wo", [128, 8 * D], BF16)
    L["CQ"] = sb("pa_cq", [128, S], BF16)
    L["SQ"] = sb("pa_sq", [128, S], BF16)
    L["CK"] = sb("pa_ck", [128, S], BF16)
    L["SK"] = sb("pa_sk", [128, S], BF16)
    L["qs_c"] = sb("pa_qsc", [128, 4], F32)
    L["fence"] = sb("pa_fence", [1, 128], BF16)
    L["trig"] = sb("pa_trig", [128, 2 * 512], F32)
    L["frs"] = sb("pa_frs", [128, 512], F32)
    L["nr"] = sb("pa_nr", [128, 512], F32)
    L["zt"] = [sb(f"pa_z{i}", [128, D], F32) for i in range(2)]
    L["zsh"] = [sb(f"pa_zs{i}", [128, 512], F32) for i in range(2)]
    L["tfm"] = sb("pa_tfm", [128, 8 * 512], BF16)
    L["qT"] = sb("pa_qT", [128, 8 * 512], BF16)
    L["kT"] = sb("pa_kT", [128, 8 * 768], BF16)
    L["VE"] = sb("pa_ve", [128, 6 * 8 * 128], BF16)
    L["sq2"] = sb("pa_sq2", [128, 2 * 512], BF16)
    L["rsq"] = sb("pa_rsq", [128, 2 * 512], BF16)
    L["tm1"] = sb("pa_tm1", [128, 512], BF16)
    L["tm2"] = sb("pa_tm2", [128, 512], BF16)
    L["pT"] = [sb(f"pa_pt{i}", [128, 1024], BF16) for i in range(2)]
    L["oT"] = sb("pa_oT", [128, 8 * 512], BF16)
    L["maxc"] = sb("pa_maxc", [128, 16], F32)
    L["max8"] = sb("pa_max8", [128, 16], F32)
    L["sums"] = sb("pa_sums", [128, 16], F32)
    L["lns"] = sb("pa_lns", [128, 16], F32)
    L["bcol"] = sb("pa_bcol", [128, 16], F32)
    L["bT"] = sb("pa_bT", [16, 128], BF16)
    L["psB"] = [psb(f"pa_psB{i}", [128, 1024]) for i in range(2)]
    L["oA"] = [psb(f"pa_oA{i}", [64, 512]) for i in range(2)]
    L["oB"] = [psb(f"pa_oB{i}", [128, 512]) for i in range(2)]
    L["vex"] = L["VE"][:].rearrange("p (k pb x) -> p k pb x", pb=8, x=128)
    L["wqv"] = L["Wq"][:].rearrange("p (dc f) -> p dc f", dc=8)
    L["wov"] = L["Wo"][:].rearrange("p (ec d) -> p ec d", ec=8)
    L["zin"], L["zout"] = c.d_z[0], c.d_z[1]

    with nc.Block(f"pa{li}t1") as blk:
        @blk.gpsimd
        def _(g):
            for col, srcd, rot in [(0, c.f_qs, 0), (1, c.f_qs, 1),
                                   (2, c.f_ks, 0), (3, c.f_ks, 1)]:
                for p0 in (0, 64):
                    if rot:
                        c.gd_a.inc(g.dma_start(
                            L["qs_c"][p0:p0 + 32, col:col + 1],
                            srcd[li][32:64, None]), 16)
                        c.gd_a.inc(g.dma_start(
                            L["qs_c"][p0 + 32:p0 + 64, col:col + 1],
                            srcd[li][0:32, None]), 16)
                    else:
                        c.gd_a.inc(g.dma_start(
                            L["qs_c"][p0:p0 + 64, col:col + 1],
                            srcd[li][:, None]), 16)
            g.memset(L["zsh"][0][0:1, :], 0.0)
            L["gd_qs"] = c.gd_a.n
            c.gd_a.inc(g.dma_start(L["wqv"],
                c.fWqkv[li].rearrange("(dc p) f -> p dc f", p=128)), 16)
            L["gd_wq"] = c.gd_a.n
            c.gd_a.inc(g.dma_start(L["wov"],
                c.fWo[li].rearrange("(ec p) d -> p ec d", p=128)), 16)
            L["gd_wo"] = c.gd_a.n

    for gi in range(8):
        B = c.bases()
        gsl = slice(512 * gi, 512 * (gi + 1))
        with nc.Block(f"pa{li}t2_{gi}") as blk:
            @blk.sync
            def _(sy):
                for dst, src in ((L["trig"][:, 0:512], c.r_cq),
                                 (L["trig"][:, 512:1024], c.r_sq),
                                 (L["frs"][:], c.r_ck),
                                 (L["nr"][:], c.r_sk)):
                    c.yd_c.inc(sy.dma_start(dst, src[:, gsl]), 16)
                c.yd_c.inc(sy.dma_start(L["fence"][0:1, :],
                                        c.d_QT[0:1, 0:128]), 16)

            @blk.vector
            def _(v):
                if gi == 0:
                    v.wait_ge(c.gd_a.h, L["gd_qs"])
                v.wait_ge(c.yd_c.h, c.yd_c.n)
                v.tensor_scalar(L["CQ"][:, gsl], L["trig"][:, 0:512],
                                L["qs_c"][:, 0:1], None, ALU.mult)
                v.tensor_scalar(L["SQ"][:, gsl], L["trig"][:, 512:1024],
                                L["qs_c"][:, 1:2], None, ALU.mult)
                v.tensor_scalar(L["CK"][:, gsl], L["frs"][:],
                                L["qs_c"][:, 2:3], None, ALU.mult)
                c.dv_a.inc(v.tensor_scalar(L["SK"][:, gsl], L["nr"][:],
                                           L["qs_c"][:, 3:4], None,
                                           ALU.mult), 1)

    for s_ in range(NSEG):
        _pa_segment(c, li, s_, L)

    if f"dbg_z1" in c.dbg and li == getattr(c, "dbg_pa_layer", 0):
        with nc.Block(f"pa{li}dbg") as blk:
            @blk.sync
            def _(sy):
                sy.wait_ge(c.yd_b.h, c.yd_b.n)
                c.yd_c.inc(sy.dma_start(c.dbg["dbg_z1"][:], c.d_z[1][:]), 16)
                sy.wait_ge(c.yd_c.h, c.yd_c.n)

    st.close()


def _pa_segment(c, li, s_, L):
    nc = c.nc
    VB = 194
    zt, zsh, tfm, qT, kT, VE = (L[k] for k in
        ["zt", "zsh", "tfm", "qT", "kT", "VE"])
    vex, wqv, wov = L["vex"], L["wqv"], L["wov"]
    CQ, SQ, CK, SK = L["CQ"], L["SQ"], L["CK"], L["SK"]
    sq2, rsq, tm1, tm2, pT, oT = (L[k] for k in
        ["sq2", "rsq", "tm1", "tm2", "pT", "oT"])
    maxc, max8, sums, lns, bcol, bT, nr = (L[k] for k in
        ["maxc", "max8", "sums", "lns", "bcol", "bT", "nr"])
    psB, oA, oB = L["psB"], L["oA"], L["oB"]
    zin, zout = L["zin"], L["zout"]
    frs = L["frs"]
    s0 = (s_ == 0)
    gsl = slice(512 * s_, 512 * (s_ + 1))

    def kcs_of(w):
        return [2, 3] if (s0 and w == 0) else [2 * w + k for k in range(4)]

    B = c.bases()
    PB = Sched(B["pe_b"])
    DD = Sched(B["dv_d"])
    PD = Sched(B["pe_d"])
    DF = Sched(B["dv_f"])
    ACA = Sched(B["ac_a"])
    ACD = Sched(B["ac_d"])
    DD.tick("halo")
    for ti in range(4):
        for hf in range(2):
            PB.tick("T", ti, hf)
            DD.tick("T", ti, hf)
    for fc in range(16):
        PB.tick("qkv", fc)
        DD.tick("qkv", fc)
    for ti in range(4):
        PB.tick("v", ti)
        DD.tick("v", ti)
    for side in ("q", "k"):
        for fc in range(8):
            DF.tick("sq2", side, fc)
            PD.tick("norm", side, fc)
        DF.tick("nmax", side)
        ACD.tick("rsq", side)
        DF.tick("rsq", side)
        for fc in range(8):
            PD.tick("bc", side, fc)
            PD.tick("rot", side, fc)
            DF.tick("app", side, fc)
    for w in range(2):
        for q_ in range(2):
            for pc in range(8):
                PD.tick("qk", w, q_, pc)
                DF.tick("max", w, q_, pc)
                ACD.tick("eqk", w, q_, pc)
                DF.tick("bias", w, q_, pc)
            PD.tick("bT", w, q_)
            DF.tick("bT", w, q_)
            for pc in range(10):
                if pc < 8:
                    PD.tick("sT", w, q_, pc)
                    ACA.tick("exp", w, q_, pc)
                if 1 <= pc <= 8:
                    PD.tick("pv", w, q_, pc - 1)
                if 2 <= pc:
                    DF.tick("oev", w, q_, pc - 2)
    for ti in range(4):
        PB.tick("wo", ti)
        DF.tick("res", ti)

    with nc.Block(f"pa{li}s{s_}") as blk:
        @blk.sync
        def _(sy):
            if s0:
                sy.wait_ge(c.yd_b.h, B["yd_b"])
            for ti in ((0, 1, 2, 3) if s0 else (2, 3)):
                tau = 4 * s_ + ti
                if ti >= 2:
                    sy.wait_ge(c.pe_b.h, PB.at("T", ti - 2, 1))
                c.yd_a.inc(sy.dma_start(
                    zt[ti % 2][:], zin[128 * tau:128 * (tau + 1), :]), 16)
                if tau == 0:
                    c.yd_a.inc(sy.dma_start(zsh[0][1:128, :],
                                            zin[0:127, 512:1024]), 16)
                else:
                    c.yd_a.inc(sy.dma_start(
                        zsh[ti % 2][:],
                        zin[128 * tau - 1:128 * (tau + 1) - 1, 512:1024]), 16)
            c.yd_a.inc(sy.dma_start(L["fence"][0:1, :],
                                    c.d_QT[0:1, 0:128]), 16)
            for ti in range(4):
                tau = 4 * s_ + ti
                sy.wait_ge(c.pe_b.h, PB.at("T", 3, 1))
                if ti >= 2:
                    sy.wait_ge(c.yd_b.h, B["yd_b"] + 16 * (ti - 1))
                c.yd_a.inc(sy.dma_start(
                    zt[ti % 2][:], zin[128 * tau:128 * (tau + 1), :]), 16)
                c.yd_a.inc(sy.dma_start(L["fence"][0:1, :],
                                        c.d_QT[0:1, 0:128]), 16)
                sy.wait_ge(c.dv_f.h, DF.at("res", ti))
                c.yd_b.inc(sy.dma_start(zout[128 * tau:128 * (tau + 1), :],
                                        zt[ti % 2][:]), 16)
            if s_ < NSEG - 1:
                for ti in range(2):
                    tau = 4 * (s_ + 1) + ti
                    sy.wait_ge(c.yd_b.h, B["yd_b"] + 16 * (ti + 3))
                    sy.wait_ge(c.pe_b.h, PB.at("T", ti + 2, 1))
                    c.yd_a.inc(sy.dma_start(
                        zt[ti % 2][:], zin[128 * tau:128 * (tau + 1), :]), 16)
                    c.yd_a.inc(sy.dma_start(
                        zsh[ti % 2][:],
                        zin[128 * tau - 1:128 * (tau + 1) - 1, 512:1024]), 16)
                c.yd_a.inc(sy.dma_start(L["fence"][0:1, :],
                                        c.d_QT[0:1, 0:128]), 16)

        @blk.tensor
        def _(t):
            for ti in range(4):
                if s0:
                    nw = (2 * ti + 4) if ti < 3 else 9
                else:
                    nw = [0, 0, 4, 5][ti]
                t.wait_ge(c.yd_a.h, B["yd_a"] + 16 * nw)
                for hf in range(2):
                    if 2 * ti + hf >= 2:
                        pt_, ph_ = divmod(2 * ti + hf - 2, 2)
                        t.wait_ge(c.dv_d.h, DD.at("T", pt_, ph_))
                    for k4 in range(4):
                        dc = 4 * hf + k4
                        srcb = (zt[ti % 2][:, 128 * dc:128 * (dc + 1)] if dc < 4
                                else zsh[ti % 2][:, 128 * (dc - 4):128 * (dc - 3)])
                        ins = nc.tensor.transpose(
                            oB[(2 * ti + hf) % 2][:, 128 * k4:128 * (k4 + 1)],
                            srcb, c.identf[:])
                    c.pe_b.inc(ins, 1)
            if s0:
                t.wait_ge(c.gd_a.h, L["gd_wq"])
            for fc in range(16):
                pq = psB[fc % 2]
                if fc < 2:
                    t.wait_ge(c.dv_d.h, DD.at("T", 3, 1))
                else:
                    t.wait_ge(c.dv_d.h, DD.at("qkv", fc - 2))
                if 8 <= fc < 10:
                    t.wait_ge(c.dv_d.h, DD.at("halo"))
                for dc in range(8):
                    ins = t.matmul(pq[:, 0:512],
                                   wqv[:, dc, 128 * fc:128 * (fc + 1)],
                                   tfm[:, 512 * dc:512 * (dc + 1)],
                                   start=(dc == 0), stop=(dc == 7))
                c.pe_b.inc(ins, 1)
            for ti in range(4):
                pv = psB[ti % 2]
                t.wait_ge(c.dv_d.h, DD.at("qkv", 14 + (ti % 2)))
                if ti >= 2:
                    t.wait_ge(c.dv_d.h, DD.at("v", ti - 2))
                for half in range(2):
                    for dc in range(8):
                        ins = t.matmul(
                            pv[:, 512 * half:512 * (half + 1)],
                            tfm[:, 512 * dc + 128 * ti:512 * dc + 128 * (ti + 1)],
                            wqv[:, dc, 2048 + 512 * half:2560 + 512 * half],
                            start=(dc == 0), stop=(dc == 7))
                c.pe_b.inc(ins, 1)
            for side, nm in [(0, "q"), (1, "k")]:
                if nm == "q":
                    t.wait_ge(c.dv_d.h, DD.at("T", 3, 1))
                for fc in range(8):
                    t.wait_ge(c.dv_f.h, DF.at("sq2", nm, fc))
                    c.pe_d.inc(t.matmul(
                        oB[side][:], c.nlut[:, 128 * fc:128 * (fc + 1)],
                        sq2[:, 512 * (fc % 2):512 * (fc % 2 + 1)],
                        start=(fc == 0), stop=(fc == 7)), 1)
                for fc in range(8):
                    t.wait_ge(c.dv_f.h, DF.at("rsq", nm))
                    if side == 0 and fc in (0, 1):
                        t.wait_ge(c.dv_f.h, DF.at("nmax", nm))
                    elif fc >= 2 or side == 1:
                        pnm, pfc = (nm, fc - 2) if fc >= 2 else ("q", 6 + fc)
                        t.wait_ge(c.dv_f.h, DF.at("app", pnm, pfc))
                    c.pe_d.inc(t.matmul(oB[fc % 2][:],
                                        c.bsel[:, 128 * fc:128 * (fc + 1)],
                                        rsq[:, 512 * side:512 * (side + 1)],
                                        start=True, stop=True), 1)
                    srcc = (qT[:, 512 * fc:512 * (fc + 1)] if side == 0 else
                            kT[:].rearrange("p (pb x) -> p pb x", pb=8)[
                                :, fc, 256:768])
                    c.pe_d.inc(t.matmul(psB[fc % 2][:, 0:512], c.Pmat[:], srcc,
                                        start=True, stop=True), 1)
            for w in range(2):
                kcs = kcs_of(w)
                nkc = len(kcs)
                klen = 128 * nkc
                for q_ in range(2):
                    for pc in range(8):
                        pq = psB[pc % 2]
                        if (w, q_) == (0, 0) and pc < 2:
                            t.wait_ge(c.dv_f.h, DF.at("app", "k", 6 + pc))
                        elif pc < 2:
                            pw, pq_ = (w, q_ - 1) if q_ else (w - 1, 1)
                            t.wait_ge(c.ac_a.h, ACA.at("exp", pw, pq_, 6 + pc))
                        else:
                            t.wait_ge(c.ac_d.h, ACD.at("eqk", w, q_, pc - 2))
                        mrhs = (c.Mqk0[:, 256 * q_:256 * (q_ + 1)] if (s0 and w == 0)
                                else c.Mqk[:, 512 * q_:512 * (q_ + 1)])
                        for hh in range(2):
                            co = 512 * hh
                            r0, r1 = (0, 64) if hh == 0 else (64, 128)
                            tp = (0, 0) if hh == 0 else (64, 0)
                            t.matmul(pq[:, co:co + klen], c.identb[:], mrhs,
                                     start=True, stop=False)
                            ins = t.matmul(
                                pq[:, co:co + klen],
                                qT[r0:r1, 512 * pc + 256 * w + 128 * q_:
                                   512 * pc + 256 * w + 128 * (q_ + 1)],
                                kT[r0:r1, 768 * pc + 128 * kcs[0]:
                                   768 * pc + 128 * kcs[0] + klen],
                                start=False, stop=True, tile_position=tp)
                        c.pe_d.inc(ins, 1)
                    t.wait_ge(c.dv_f.h, DF.at("bias", w, q_, 7))
                    if (w, q_) != (0, 0):
                        pw, pq_ = (w, q_ - 1) if q_ else (w - 1, 1)
                        t.wait_ge(c.dv_f.h, DF.at("oev", pw, pq_, 7))
                    c.pe_d.inc(nc.tensor.transpose(
                        oA[0][0:16, 128:256], bcol[:], c.identf[:]), 1)
                    for pc in range(10):
                        if pc < 8:
                            ps = psB[pc % 2]
                            t.wait_ge(c.dv_f.h, DF.at("bT", w, q_))
                            if pc >= 2:
                                t.wait_ge(c.ac_a.h, ACA.at("exp", w, q_, pc - 2))
                            else:
                                t.wait_ge(c.ac_d.h, ACD.at("eqk", w, q_, 6 + pc))
                            mt = (c.MT0[:, 256 * q_:256 * (q_ + 1)] if (s0 and w == 0)
                                  else c.MT[:, 512 * q_:512 * (q_ + 1)])
                            for hh in range(2):
                                co = 512 * hh
                                r0, r1 = (0, 64) if hh == 0 else (64, 128)
                                tp = (0, 0) if hh == 0 else (64, 0)
                                h = 2 * pc + hh
                                t.matmul(ps[:, co:co + klen], c.identb[:], mt,
                                         start=True, stop=False)
                                for ci in range(nkc):
                                    t.matmul(ps[:, co + 128 * ci:co + 128 * (ci + 1)],
                                             c.sel16[:, 128 * h:128 * (h + 1)],
                                             bT[0:16, 0:128],
                                             start=False, stop=False)
                                for ci, kc in enumerate(kcs):
                                    ins = t.matmul(
                                        ps[:, co + 128 * ci:co + 128 * (ci + 1)],
                                        kT[r0:r1, 768 * pc + 128 * kc:
                                           768 * pc + 128 * (kc + 1)],
                                        qT[r0:r1, 512 * pc + 256 * w + 128 * q_:
                                           512 * pc + 256 * w + 128 * (q_ + 1)],
                                        start=False, stop=(ci == nkc - 1),
                                        tile_position=tp)
                            c.pe_d.inc(ins, 1)
                        if 1 <= pc <= 8:
                            ppc = pc - 1
                            t.wait_ge(c.ac_a.h, ACA.at("exp", w, q_, ppc))
                            if (w, q_) == (0, 0) and ppc < 2:
                                t.wait_ge(c.dv_f.h, DF.at("app", "k", 6 + ppc))
                            elif ppc >= 2:
                                t.wait_ge(c.dv_f.h, DF.at("oev", w, q_, ppc - 2))
                            else:
                                pw, pq_ = (w, q_ - 1) if q_ else (w - 1, 1)
                                t.wait_ge(c.dv_f.h,
                                          DF.at("oev", pw, pq_, 6 + ppc))
                            for ci, kc in enumerate(kcs):
                                t.matmul(oA[ppc % 2][:, 0:128],
                                         vex[:, kc, ppc, 0:64],
                                         pT[ppc % 2][:, 128 * ci:128 * (ci + 1)],
                                         start=(ci == 0), stop=(ci == nkc - 1))
                                ins = t.matmul(
                                    oB[ppc % 2][:, 0:128],
                                    vex[:, kc, ppc, 0:128],
                                    pT[ppc % 2][:, 512 + 128 * ci:512 + 128 * (ci + 1)],
                                    start=(ci == 0), stop=(ci == nkc - 1))
                            c.pe_d.inc(ins, 1)
            if s0:
                t.wait_ge(c.gd_a.h, L["gd_wo"])
            for ti in range(4):
                pw_ = psB[ti % 2]
                t.wait_ge(c.dv_f.h, DF.at("oev", 1, 1, 7))
                if ti >= 2:
                    t.wait_ge(c.dv_f.h, DF.at("res", ti - 2))
                for half in range(2):
                    for ec in range(8):
                        ins = t.matmul(
                            pw_[:, 512 * half:512 * (half + 1)],
                            oT[:, 512 * ec + 128 * ti:512 * ec + 128 * (ti + 1)],
                            wov[:, ec, 512 * half:512 * (half + 1)],
                            start=(ec == 0), stop=(ec == 7))
                c.pe_b.inc(ins, 1)

        @blk.vector
        def _(v):
            if s0:
                c.dv_d.inc(v.tensor_copy(nr[:, 0:1], c.iota_c[:]), 1)
            else:
                v.tensor_copy(
                    kT[:].rearrange("p (pb x) -> p pb x", pb=8)[:, :, 0:256],
                    kT[:].rearrange("p (pb x) -> p pb x", pb=8)[:, :, 512:768])
                c.dv_d.inc(v.tensor_copy(VE[:, 0:2048], VE[:, 4096:6144]), 1)
            for ti in range(4):
                for hf in range(2):
                    v.wait_ge(c.pe_b.h, PB.at("T", ti, hf))
                    dst = tfm[:].rearrange("p (dc x) -> p dc x", dc=8)[
                        :, 4 * hf:4 * hf + 4, 128 * ti:128 * (ti + 1)]
                    c.dv_d.inc(v.tensor_copy(
                        dst, oB[(2 * ti + hf) % 2][:].rearrange(
                            "p (k x) -> p k x", k=4)), 1)
            for fc in range(16):
                v.wait_ge(c.pe_b.h, PB.at("qkv", fc))
                if fc < 8:
                    dst = qT[:, 512 * fc:512 * (fc + 1)]
                else:
                    dst = kT[:].rearrange("p (pb x) -> p pb x", pb=8)[
                        :, fc - 8, 256:768]
                c.dv_d.inc(v.tensor_copy(dst, psB[fc % 2][:, 0:512]), 1)
            for ti in range(4):
                v.wait_ge(c.pe_b.h, PB.at("v", ti))
                c.dv_d.inc(v.tensor_copy(
                    VE[:, 1024 * (2 + ti):1024 * (3 + ti)],
                    psB[ti % 2][:, 0:1024]), 1)
            for side, nm in [(0, "q"), (1, "k")]:
                for fc in range(8):
                    v.wait_ge(c.dv_d.h, DD.at("qkv", fc if side == 0 else 8 + fc))
                    if (side, fc) not in ((0, 0), (0, 1)):
                        pnm, pfc = (nm, fc - 2) if fc >= 2 else ("q", 6 + fc)
                        v.wait_ge(c.pe_d.h, PD.at("norm", pnm, pfc))
                    if side == 0:
                        srcq = qT[:, 512 * fc:512 * (fc + 1)]
                    else:
                        srcq = kT[:].rearrange("p (pb x) -> p pb x", pb=8)[
                            :, fc, 256:768]
                    c.dv_f.inc(v.tensor_tensor(
                        sq2[:, 512 * (fc % 2):512 * (fc % 2 + 1)],
                        srcq, srcq, ALU.mult), 1)
                v.wait_ge(c.pe_d.h, PD.at("norm", nm, 7))
                c.dv_f.inc(v.tensor_scalar(nr[:], oB[side][:], 1e-30, None,
                                           ALU.max), 1)
                v.wait_ge(c.ac_d.h, ACD.at("rsq", nm))
                v.tensor_scalar(frs[:], frs[:], 1e-12, None, ALU.max)
                with nc.allow_low_precision(reason="rsq feeds bf16 matmul"):
                    c.dv_f.inc(v.reciprocal(rsq[:, 512 * side:512 * (side + 1)],
                                            frs[:]), 1)
                TBL = (CQ, SQ) if side == 0 else (CK, SK)
                for fc in range(8):
                    v.wait_ge(c.pe_d.h, PD.at("rot", nm, fc))
                    if side == 0:
                        srcc = qT[:, 512 * fc:512 * (fc + 1)]
                    else:
                        srcc = kT[:].rearrange("p (pb x) -> p pb x", pb=8)[
                            :, fc, 256:768]
                    v.tensor_tensor(tm1[:], srcc, TBL[0][:, gsl], ALU.mult)
                    v.tensor_tensor(tm2[:], psB[fc % 2][:, 0:512],
                                    TBL[1][:, gsl], ALU.mult)
                    v.tensor_tensor(tm1[:], tm1[:], tm2[:], ALU.add)
                    c.dv_f.inc(v.tensor_tensor(srcc, tm1[:], oB[fc % 2][:],
                                               ALU.mult), 1)
            for w in range(2):
                kcs = kcs_of(w)
                klen = 128 * len(kcs)
                for q_ in range(2):
                    for pc in range(8):
                        v.wait_ge(c.pe_d.h, PD.at("qk", w, q_, pc))
                        gv = psB[pc % 2][:].rearrange("p (g x) -> p g x", g=2)
                        v.tensor_reduce(maxc[:, 2 * pc:2 * pc + 2],
                                        gv[:, :, 0:klen], AX.X, ALU.max,
                                        negate=True)
                        v.tensor_copy(nr[:, 0:1], c.iota_c[:])
                        c.dv_f.inc(v.tensor_scalar(
                            max8[:, 2 * pc:2 * pc + 2],
                            maxc[:, 2 * pc:2 * pc + 2], 8.0, None, ALU.mult), 1)
                        v.wait_ge(c.ac_d.h, ACD.at("eqk", w, q_, pc))
                        c.dv_f.inc(v.scalar_tensor_tensor(
                            bcol[:, 2 * pc:2 * pc + 2],
                            lns[:, 2 * pc:2 * pc + 2], -0.125,
                            maxc[:, 2 * pc:2 * pc + 2], ALU.mult, ALU.add), 1)
                    v.wait_ge(c.pe_d.h, PD.at("bT", w, q_))
                    c.dv_f.inc(v.tensor_copy(bT[0:16, 0:128],
                                             oA[0][0:16, 128:256]), 1)
                    for pc in range(8):
                        v.wait_ge(c.pe_d.h, PD.at("pv", w, q_, pc))
                        od = oT[:, 512 * pc + 256 * w + 128 * q_:
                                512 * pc + 256 * w + 128 * (q_ + 1)]
                        v.tensor_copy(od[0:64, :], oA[pc % 2][0:64, 0:128])
                        c.dv_f.inc(v.tensor_copy(od[64:128, :],
                                                 oB[pc % 2][64:128, 0:128]), 1)
            for ti in range(4):
                v.wait_ge(c.pe_b.h, PB.at("wo", ti))
                v.wait_ge(c.yd_a.h,
                          B["yd_a"] + 16 * ((9 if s0 else 5) + 2 * (ti + 1)))
                c.dv_f.inc(v.tensor_tensor(zt[ti % 2][:], zt[ti % 2][:],
                                           psB[ti % 2][:], ALU.add), 1)

        @blk.scalar
        def _(sc):
            for side, nm in [(0, "q"), (1, "k")]:
                sc.wait_ge(c.dv_f.h, DF.at("nmax", nm))
                sc.activation(frs[:], nr[:], AF.Ln, bias=0.0, scale=1.0)
                sc.activation(frs[:], frs[:], AF.Exp, bias=0.0,
                              scale=c.cst[:, 10:11])
                c.ac_d.inc(sc.drain(), 1)
            for w in range(2):
                kcs = kcs_of(w)
                klen = 128 * len(kcs)
                for q_ in range(2):
                    for pc in range(8):
                        sc.wait_ge(c.dv_f.h, DF.at("max", w, q_, pc))
                        for hh in range(2):
                            h = 2 * pc + hh
                            sc.activation(
                                (tm1 if hh == 0 else tm2)[:, 0:klen],
                                psB[pc % 2][:, 512 * hh:512 * hh + klen],
                                AF.Exp, bias=max8[:, h:h + 1],
                                scale=c.cst[:, 4:5],
                                accum_out=sums[:, h:h + 1])
                        sc.activation(lns[:, 2 * pc:2 * pc + 2],
                                      sums[:, 2 * pc:2 * pc + 2], AF.Ln,
                                      bias=0.0, scale=1.0)
                        c.ac_d.inc(sc.drain(), 1)
                    for pc in range(8):
                        sc.wait_ge(c.pe_d.h, PD.at("sT", w, q_, pc))
                        if pc >= 2:
                            sc.wait_ge(c.pe_d.h, PD.at("pv", w, q_, pc - 2))
                        pv = pT[pc % 2][:].rearrange("p (g x) -> p g x", g=2)
                        gv = psB[pc % 2][:].rearrange("p (g x) -> p g x", g=2)
                        sc.activation(pv[:, :, 0:klen], gv[:, :, 0:klen],
                                      AF.Exp, bias=0.0, scale=c.cst[:, 4:5])
                        c.ac_a.inc(sc.drain(), 1)

    return


def _prev_iter(w, q_, pc):
    i = ((w * 2 + q_) * 8 + pc) - 2
    return i // 16, (i // 8) % 2, i % 8


def pb_inner_ffn(c, li, final=False):
    nc = c.nc
    st = contextlib.ExitStack()

    def sb(name, shape, dt):
        return st.enter_context(nc.sbuf_tensor(f"{name}_{li}", shape, dt))

    def psb(name, shape, dt=F32):
        return st.enter_context(nc.psum_tensor(f"{name}_{li}", shape, dt))

    W1 = sb("pb_w1", [128, 8 * 2 * DI], BF16)
    W2 = sb("pb_w2", [128, NJC * D], BF16)
    b2r = sb("pb_b2", [1, D], BF16)
    gcol = sb("pb_gc", [128, 8], F32)
    bcol = sb("pb_bc", [128, 8], F32)
    b1c = sb("pb_b1c", [128, 44], F32)

    fence = sb("pb_fence", [1, 128], BF16)
    zt = [sb(f"pb_z{i}", [128, D], F32) for i in range(2)]
    zsh = [sb(f"pb_zs{i}", [128, 512], F32) for i in range(2)]
    L_zsh = zsh
    xn = sb("pb_xn", [128, D], F32)
    xfm = sb("pb_xfm", [128, 8 * 512], BF16)
    hsb = sb("pb_h", [128, NJC * 512], BF16)
    gg = sb("pb_gg", [128, 512], BF16)
    stat = sb("pb_stat", [128, 16], F32)
    sq = sb("pb_sq", [128, D], F32)
    zo2 = [sq, sq] if final else None
    x1t = [xn, xn] if final else None
    W1v = W1[:].rearrange("p (dc f) -> p dc f", dc=8)

    psB = [psb(f"pb_psB{i}", [128, 1024]) for i in range(2)]
    pw2 = [psb(f"pb_pw2{i}", [128, 1024]) for i in range(2)]

    zin, zout = c.d_z[1], c.d_z[0]

    B = c.bases()
    rstage = sb("pb_rst", [128, 128], BF16)
    gdc = {}
    with nc.Block(f"pb{li}w") as blk:
        @blk.gpsimd
        def _(g):
            c.gd_a.inc(g.dma_start(c.d_rows[:].rearrange("a b c -> (a b c)"),
                                   c.zero_rows[:]), 16)
            g.wait_ge(c.gd_a.h, c.gd_a.n)
            c.gd_a.inc(g.dma_start(
                c.d_rows[0].rearrange("a b -> (a b)")[0:2730],
                c.f_b1[li][0:2730]), 16)
            c.gd_a.inc(g.dma_start(
                c.d_rows[1].rearrange("a b -> (a b)")[0:2730],
                c.f_b1[li][2730:5460]), 16)
            c.gd_a.inc(g.dma_start(
                c.d_rows[2].rearrange("a b -> (a b)")[0:1024],
                c.f_lng[li][:]), 16)
            c.gd_a.inc(g.dma_start(
                c.d_rows[3].rearrange("a b -> (a b)")[0:1024],
                c.f_lnb[li][:]), 16)
            gdc["rows"] = c.gd_a.n
            g.memset(zsh[0][0:1, :], 0.0)
            c.gd_a.inc(g.dma_start(b2r[0:1, :], c.f_b2[li][None, :]), 16)
            c.gd_a.inc(g.dma_start(
                W1[:].rearrange("p (dc f) -> p dc f", dc=8),
                c.fW1[li].rearrange("(dc p) f -> p dc f", p=128)), 16)
            gdc["w1"] = c.gd_a.n
            c.gd_a.inc(g.dma_start(
                W2[:].rearrange("p (jc d) -> p jc d", jc=NJC)[:, 0:21, :],
                c.fW2[li][0:2688].rearrange("(jc p) d -> p jc d", p=128)), 16)
            c.gd_a.inc(g.dma_start(
                W2[0:JLAST, 21 * D:22 * D], c.fW2[li][2688:2730, :]), 16)
            gdc["w2"] = c.gd_a.n

        @blk.sync
        def _(sy):
            sy.wait_ge(c.gd_a.h, gdc["rows"])
            for i in range(4):
                c.yd_c.inc(sy.dma_start(rstage[:, 32 * i:32 * (i + 1)],
                                        c.d_rows[i], transpose=True), 16)
            c.yd_c.inc(sy.dma_start(fence[0:1, :], c.d_QT[0:1, 0:128]), 16)

        @blk.vector
        def _(v):
            v.wait_ge(c.yd_c.h, c.yd_c.n)
            v.tensor_copy(b1c[:, 0:22], rstage[:, 0:22])
            v.tensor_copy(b1c[:, 22:44], rstage[:, 32:54])
            v.tensor_copy(gcol[:], rstage[:, 64:72])
            c.dv_a.inc(v.tensor_copy(bcol[:], rstage[:, 96:104]), 1)

    for s_ in range(NSEG):
        _pb_segment(c, li, s_, final, locals())

    if "dbg_z2" in c.dbg and not final:
        with nc.Block(f"pb{li}dbg") as blk:
            @blk.sync
            def _(sy):
                sy.wait_ge(c.yd_b.h, c.yd_b.n)
                c.yd_c.inc(sy.dma_start(c.dbg["dbg_z2"][:], c.d_z[0][:]), 16)
                sy.wait_ge(c.yd_c.h, c.yd_c.n)

    st.close()


def _pb_segment(c, li, s_, final, L):
    nc = c.nc
    zt, zsh, xn, xfm, hsb, gg, stat, sq = (L[k] for k in
        ["zt", "zsh", "xn", "xfm", "hsb", "gg", "stat", "sq"])
    W1v, W2, b2r, gcol, bcol, b1c = (L[k] for k in
        ["W1v", "W2", "b2r", "gcol", "bcol", "b1c"])
    psB, pw2 = L["psB"], L["pw2"]
    zin, zout = L["zin"], L["zout"]
    zo2, x1t = L["zo2"], L["x1t"]
    fence = L["fence"]
    gdc = L["gdc"]
    s0 = (s_ == 0)

    B = c.bases()
    PB = Sched(B["pe_b"])
    PD = Sched(B["pe_d"])
    DD = Sched(B["dv_d"])
    DF = Sched(B["dv_f"])
    ACA = Sched(B["ac_a"])
    ACD = Sched(B["ac_d"])
    for ti in range(4):
        DF.tick("st1", ti)
        ACD.tick("rstd", ti)
        DF.tick("bias", ti)
        ACD.tick("apply", ti)
        for hf in range(2):
            PB.tick("T", ti, hf)
        if ti >= 1:
            for hf in range(2):
                DD.tick("T", ti - 1, hf)
    for hf in range(2):
        DD.tick("T", 3, hf)
    for jc in range(NJC):
        PD.tick("w1", jc)
        ACA.tick("gelu", jc)
        DF.tick("h", jc)
    for ti in range(4):
        PB.tick("w2", ti)
        DF.tick("res", ti)
        if final:
            DF.tick("st2", ti)
            ACD.tick("rstd2", ti)
            DF.tick("bias2", ti)
            ACD.tick("yn", ti)
            DF.tick("out", ti)

    with nc.Block(f"pb{li}s{s_}") as blk:
        if s_ == 0 and li == 0 and "dbg_pbx" in c.dbg:
            @blk.gpsimd
            def _(g):
                g.wait_ge(c.ac_d.h, ACD.at("apply", 0))
                dbx = c.dbg["dbg_pbx"]
                c.gd_b.inc(g.dma_start(dbx[384:512, 0:16], L["stat"][:]), 16)
                c.gd_b.inc(g.dma_start(dbx[256:384, :], L["xn"][:]), 16)
                c.gd_b.inc(g.dma_start(dbx[128:256, 0:512], L["zsh"][0][:]), 16)
                c.gd_b.inc(g.dma_start(dbx[0:128, :], L["zt"][0][:]), 16)
                g.wait_ge(c.gd_b.h, c.gd_b.n)

        @blk.sync
        def _(sy):
            if s0:
                sy.wait_ge(c.yd_b.h, B["yd_b"])
            for ti in ((0, 1, 2, 3) if s0 else (2, 3)):
                tau = 4 * s_ + ti
                if ti >= 2:
                    sy.wait_ge(c.ac_d.h, ACD.at("apply", ti - 2))
                c.yd_a.inc(sy.dma_start(
                    zt[ti % 2][:], zin[128 * tau:128 * (tau + 1), :]), 16)
                if tau == 0:
                    c.yd_a.inc(sy.dma_start(zsh[0][1:128, :],
                                            zin[0:127, 512:1024]), 16)
                else:
                    c.yd_a.inc(sy.dma_start(
                        zsh[ti % 2][:],
                        zin[128 * tau - 1:128 * (tau + 1) - 1, 512:1024]), 16)
            c.yd_a.inc(sy.dma_start(fence[0:1, :], c.d_QT[0:1, 0:128]), 16)
            for ti in range(4):
                tau = 4 * s_ + ti
                sy.wait_ge(c.pe_b.h, PB.at("T", 3, 1))
                if ti >= 2:
                    sy.wait_ge(c.yd_b.h, B["yd_b"] + 16 * (ti - 1))
                c.yd_a.inc(sy.dma_start(
                    zt[ti % 2][:], zin[128 * tau:128 * (tau + 1), :]), 16)
                if final:
                    c.yd_a.inc(sy.dma_start(
                        x1t[ti % 2][:], c.d_x1[128 * tau:128 * (tau + 1), :]), 16)
                c.yd_a.inc(sy.dma_start(fence[0:1, :],
                                        c.d_QT[0:1, 0:128]), 16)
                if final:
                    sy.wait_ge(c.dv_f.h, DF.at("out", ti))
                    c.yd_b.inc(sy.dma_start(
                        c.out[128 * tau:128 * (tau + 1), :], zo2[ti % 2][:]), 16)
                else:
                    sy.wait_ge(c.dv_f.h, DF.at("res", ti))
                    c.yd_b.inc(sy.dma_start(
                        zout[128 * tau:128 * (tau + 1), :], zt[ti % 2][:]), 16)
            if s_ < NSEG - 1:
                for ti in range(2):
                    tau = 4 * (s_ + 1) + ti
                    sy.wait_ge(c.yd_b.h, B["yd_b"] + 16 * (ti + 3))
                    sy.wait_ge(c.ac_d.h, ACD.at("apply", ti + 2))
                    c.yd_a.inc(sy.dma_start(
                        zt[ti % 2][:], zin[128 * tau:128 * (tau + 1), :]), 16)
                    c.yd_a.inc(sy.dma_start(
                        zsh[ti % 2][:],
                        zin[128 * tau - 1:128 * (tau + 1) - 1, 512:1024]), 16)
                c.yd_a.inc(sy.dma_start(fence[0:1, :],
                                        c.d_QT[0:1, 0:128]), 16)

        @blk.vector
        def _(v):
            def _evacs(ti):
                for hf in range(2):
                    v.wait_ge(c.pe_b.h, PB.at("T", ti, hf))
                    for k4 in range(4):
                        dc = 4 * hf + k4
                        ins = v.tensor_scalar(
                            xfm[:, 512 * dc + 128 * ti:512 * dc + 128 * (ti + 1)],
                            pw2[(2 * ti + hf) % 2][:, 128 * k4:128 * (k4 + 1)],
                            gcol[:, dc:dc + 1], bcol[:, dc:dc + 1],
                            ALU.mult, ALU.add)
                    c.dv_d.inc(ins, 1)
            for ti in range(4):
                if s0:
                    nw = (2 * ti + 4) if ti < 3 else 9
                else:
                    nw = [0, 0, 4, 5][ti]
                v.wait_ge(c.yd_a.h, B["yd_a"] + 16 * nw)
                if final and ti == 0:
                    v.wait_ge(c.yd_b.h, B["yd_b"])
                zl = zt[ti % 2][:, 0:512]
                zh = zsh[ti % 2][:]
                v.tensor_reduce(stat[:, 0:1], zl, AX.X, ALU.add)
                v.tensor_reduce(stat[:, 1:2], zh, AX.X, ALU.add)
                v.scalar_tensor_tensor(sq[:, 0:512], zl, 1.0, zl, ALU.mult,
                                       ALU.mult, accum_out=stat[:, 2:3])
                v.tensor_tensor(stat[:, 0:1], stat[:, 0:1], stat[:, 1:2], ALU.add)
                v.scalar_tensor_tensor(sq[:, 0:512], zh, 1.0, zh, ALU.mult,
                                       ALU.mult, accum_out=stat[:, 3:4])
                v.tensor_scalar(stat[:, 4:5], stat[:, 0:1], 1.0 / D, None, ALU.mult)
                v.tensor_tensor(stat[:, 2:3], stat[:, 2:3], stat[:, 3:4], ALU.add)
                v.tensor_tensor(stat[:, 6:7], stat[:, 4:5], stat[:, 4:5], ALU.mult)
                v.tensor_scalar(stat[:, 5:6], stat[:, 2:3], 1.0 / D, None, ALU.mult)
                v.tensor_copy(sq[:, 0:1], c.iota_c[:])
                c.dv_f.inc(v.tensor_tensor(stat[:, 5:6], stat[:, 5:6],
                                           stat[:, 6:7], ALU.subtract), 1)
                v.wait_ge(c.ac_d.h, ACD.at("rstd", ti))
                c.dv_f.inc(v.tensor_scalar(stat[:, 8:9], stat[:, 4:5],
                                           stat[:, 7:8], -1.0, ALU.mult,
                                           ALU.mult), 1)
                if ti >= 1:
                    _evacs(ti - 1)
            _evacs(3)
            for jc in range(NJC):
                klen = JLAST if jc == NJC - 1 else 128
                v.wait_ge(c.ac_a.h, ACA.at("gelu", jc))
                c.dv_f.inc(v.scalar_tensor_tensor(
                    hsb[0:klen, 512 * jc:512 * (jc + 1)],
                    psB[jc % 2][0:klen, 0:512], b1c[0:klen, jc:jc + 1],
                    gg[0:klen, :], ALU.add, ALU.mult), 1)
            for ti in range(4):
                v.wait_ge(c.pe_b.h, PB.at("w2", ti))
                kit = 3 if final else 2
                nfl = 9 if s0 else 5
                v.wait_ge(c.yd_a.h, B["yd_a"] + 16 * (nfl + kit * (ti + 1)))
                c.dv_f.inc(v.tensor_tensor(zt[ti % 2][:], zt[ti % 2][:],
                                           pw2[ti % 2][:], ALU.add), 1)
                if final:
                    zr = zt[ti % 2]
                    v.wait_ge(c.yd_b.h, B["yd_b"] + 16 * ti)
                    v.tensor_reduce(stat[:, 0:1], zr[:], AX.X, ALU.add)
                    v.scalar_tensor_tensor(sq[:], zr[:], 1.0, zr[:], ALU.mult,
                                           ALU.mult, accum_out=stat[:, 2:3])
                    v.tensor_scalar(stat[:, 4:5], stat[:, 0:1], 1.0 / D, None,
                                    ALU.mult)
                    v.tensor_scalar(stat[:, 5:6], stat[:, 2:3], 1.0 / D, None,
                                    ALU.mult)
                    v.tensor_tensor(stat[:, 6:7], stat[:, 4:5], stat[:, 4:5],
                                    ALU.mult)
                    v.tensor_copy(sq[:, 1:2], c.iota_c[:])
                    c.dv_f.inc(v.tensor_tensor(stat[:, 5:6], stat[:, 5:6],
                                               stat[:, 6:7], ALU.subtract), 1)
                    v.wait_ge(c.ac_d.h, ACD.at("rstd2", ti))
                    c.dv_f.inc(v.tensor_scalar(stat[:, 8:9], stat[:, 4:5],
                                               stat[:, 7:8], -1.0, ALU.mult,
                                               ALU.mult), 1)
                    v.wait_ge(c.ac_d.h, ACD.at("yn", ti))
                    v.wait_ge(c.yd_a.h,
                              B["yd_a"] + 16 * ((9 if s0 else 5) + 3 * (ti + 1)))
                    c.dv_f.inc(v.tensor_tensor(zo2[ti % 2][:], zo2[ti % 2][:],
                                               x1t[ti % 2][:], ALU.add), 1)

        @blk.scalar
        def _(sc):
            for ti in range(4):
                sc.wait_ge(c.dv_f.h, DF.at("st1", ti))
                sc.activation(stat[:, 7:8], stat[:, 5:6], AF.Ln,
                              bias=c.cst[:, 0:1], scale=1.0)
                sc.activation(stat[:, 7:8], stat[:, 7:8], AF.Exp, bias=0.0,
                              scale=c.cst[:, 1:2])
                c.ac_d.inc(sc.drain(), 1)
                sc.wait_ge(c.dv_f.h, DF.at("bias", ti))
                if ti >= 1:
                    sc.wait_ge(c.pe_b.h, PB.at("T", ti - 1, 1))
                sc.activation(xn[:, 0:512], zt[ti % 2][:, 0:512], AF.Identity,
                              bias=stat[:, 8:9], scale=stat[:, 7:8])
                sc.activation(xn[:, 512:1024], zsh[ti % 2][:], AF.Identity,
                              bias=stat[:, 8:9], scale=stat[:, 7:8])
                c.ac_d.inc(sc.drain(), 1)
            for jc in range(NJC):
                klen = JLAST if jc == NJC - 1 else 128
                sc.wait_ge(c.pe_d.h, PD.at("w1", jc))
                if jc >= 1:
                    sc.wait_ge(c.dv_f.h, DF.at("h", jc - 1))
                sc.activation(gg[0:klen, :], psB[jc % 2][0:klen, 512:1024],
                              AF.Gelu, bias=b1c[0:klen, 22 + jc:23 + jc],
                              scale=1.0)
                c.ac_a.inc(sc.drain(), 1)
            if final:
                for ti in range(4):
                    sc.wait_ge(c.dv_f.h, DF.at("st2", ti))
                    sc.activation(stat[:, 7:8], stat[:, 5:6], AF.Ln,
                                  bias=c.cst[:, 0:1], scale=1.0)
                    sc.activation(stat[:, 7:8], stat[:, 7:8], AF.Exp, bias=0.0,
                                  scale=c.cst[:, 1:2])
                    c.ac_d.inc(sc.drain(), 1)
                    sc.wait_ge(c.dv_f.h, DF.at("bias2", ti))
                    sc.wait_ge(c.yd_b.h, B["yd_b"] + 16 * ti)
                    sc.activation(zo2[ti % 2][:], zt[ti % 2][:], AF.Identity,
                                  bias=stat[:, 8:9], scale=stat[:, 7:8])
                    c.ac_d.inc(sc.drain(), 1)

        @blk.tensor
        def _(t):
            for ti in range(4):
                t.wait_ge(c.ac_d.h, ACD.at("apply", ti))
                for hf in range(2):
                    if 2 * ti + hf >= 2:
                        pt_, ph_ = divmod(2 * ti + hf - 2, 2)
                        t.wait_ge(c.dv_d.h, DD.at("T", pt_, ph_))
                    for k4 in range(4):
                        dc = 4 * hf + k4
                        ins = nc.tensor.transpose(
                            pw2[(2 * ti + hf) % 2][:, 128 * k4:128 * (k4 + 1)],
                            xn[:, 128 * dc:128 * (dc + 1)], c.identf[:])
                    c.pe_b.inc(ins, 1)
            if s0:
                t.wait_ge(c.gd_a.h, gdc["w1"])
            for jc in range(NJC):
                klen = JLAST if jc == NJC - 1 else 128
                t.wait_ge(c.dv_d.h, DD.at("T", 3, 1))
                if jc >= 2:
                    t.wait_ge(c.dv_f.h, DF.at("h", jc - 2))
                pq = psB[jc % 2]
                for part, c0 in ((0, 128 * jc), (512, DI + 128 * jc)):
                    for dc in range(8):
                        ins = t.matmul(pq[0:klen, part:part + 512],
                                       W1v[:, dc, c0:c0 + klen],
                                       xfm[:, 512 * dc:512 * (dc + 1)],
                                       start=(dc == 0), stop=(dc == 7))
                c.pe_d.inc(ins, 1)
            if s0:
                t.wait_ge(c.gd_a.h, gdc["w2"])
            for ti in range(4):
                t.wait_ge(c.dv_d.h, DD.at("T", 3, 1))
                if True:
                    pass
                pw = pw2[ti % 2]
                if ti < 2:
                    t.wait_ge(c.dv_d.h, DD.at("T", 3, 1))
                else:
                    t.wait_ge(c.dv_f.h, DF.at("res", ti - 2))
                for half in range(2):
                    for jc in range(NJC):
                        klen = JLAST if jc == NJC - 1 else 128
                        t.wait_ge(c.dv_f.h, DF.at("h", jc))
                        t.matmul(pw[:, 512 * half:512 * (half + 1)],
                                 hsb[0:klen, 512 * jc + 128 * ti:
                                     512 * jc + 128 * (ti + 1)],
                                 W2[0:klen, D * jc + 512 * half:
                                    D * jc + 512 * half + 512],
                                 start=(jc == 0), stop=False)
                    ins = t.matmul(pw[:, 512 * half:512 * (half + 1)],
                                   c.ones_row[0:1, 0:128],
                                   b2r[0:1, 512 * half:512 * half + 512],
                                   start=False, stop=True)
                c.pe_b.inc(ins, 1)

    return


def _final_wait(c):
    nc = c.nc
    with nc.Block("fin") as blk:
        @blk.sync
        def _(sy):
            sy.wait_ge(c.yd_a.h, c.yd_a.n)
            sy.wait_ge(c.yd_b.h, c.yd_b.n)
            sy.wait_ge(c.yd_c.h, c.yd_c.n)
            sy.wait_ge(c.gd_a.h, c.gd_a.n)


def _in_map(inputs, core):
    m = {
        "x": np.ascontiguousarray(inputs["L0_attn_x"][core]),
        "W_qkv": inputs["L0_attn_W_qkv"],
        "b_qkv": inputs["L0_attn_b_qkv"],
        "W_o": inputs["L0_attn_W_o"],
        "b_o": inputs["L0_attn_b_o"],
        "fWqkv": inputs["L0_ffn_Wqkv"],
        "f_qs": inputs["L0_ffn_q_scale"],
        "f_ks": inputs["L0_ffn_k_scale"],
        "fWo": inputs["L0_ffn_Wo"],
        "f_lng": inputs["L0_ffn_ln_g"],
        "f_lnb": inputs["L0_ffn_ln_b"],
        "fW1": inputs["L0_ffn_W1"],
        "f_b1": inputs["L0_ffn_b1"],
        "fW2": inputs["L0_ffn_W2"],
        "zero_rows": np.zeros(4 * 32 * 128, dtype=np.float32),
        "f_b2": inputs["L0_ffn_b2"],
    }
    m.update(_rope_tables())
    return {k: np.ascontiguousarray(np.asarray(v, dtype=np.float32)) for k, v in m.items()}


_ROPE_CACHE = {}


def _rope_tables():
    if _ROPE_CACHE:
        return _ROPE_CACHE
    half = np.arange(0, HD, 2, dtype=np.float64)
    inv_freq = 1.0 / 10000.0 ** (half / HD)
    t = np.arange(S, dtype=np.float64)
    freqs = t[:, None] * inv_freq[None, :]
    base_scale = (half + 0.4 * HD) / (1.4 * HD)
    power = (t - S // 2) / 128.0
    scale = base_scale[None, :] ** power[:, None]
    cos, sin = np.cos(freqs), np.sin(freqs)
    e = np.arange(128) % HD
    e32 = e % 32
    sgn = np.where(e < 32, -1.0, 1.0)
    _ROPE_CACHE.update({
        "r_cq": (cos * scale).T[e32],
        "r_sq": (sin * scale).T[e32] * sgn[:, None],
        "r_ck": (cos / scale).T[e32],
        "r_sk": (sin / scale).T[e32] * sgn[:, None],
    })
    return _ROPE_CACHE


def kernel(**inputs):
    nc, c = build_nc()
    in_maps = [_in_map(inputs, core) for core in range(8)]
    res = run_bass_kernel_spmd(nc, in_maps, core_ids=list(range(8)))
    return np.stack([res.results[i]["out"] for i in range(8)], axis=0)
